# version 14
# baseline (speedup 1.0000x reference)
import numpy as np
import ml_dtypes
from contextlib import ExitStack

import concourse.bass as bass
import concourse.mybir as mybir
from concourse.bass_utils import run_bass_kernel_spmd

F32 = mybir.dt.float32
BF16 = mybir.dt.bfloat16
I32 = mybir.dt.int32
AF = mybir.ActivationFunctionType
ALU = mybir.AluOpType
AXL = mybir.AxisListType

L = 2048
D = 1024
ED = 2048
NST = 16
KCV = 4
R = 64
IFF = 4096
TC = 512
NCH = L // TC
EPS = 1e-5
NE = ED // 128
NDC = D // 128
NIC = IFF // 128
NGRP = 8


class S:
    def __init__(self, h):
        self.h = h
        self.n = 0
        self.last_sw = 0

    def inc(self, k=1):
        self.n += k
        return self.n


def build_nc():
    nc = bass.Bass("TRN2", target_bir_lowering=False, detect_race_conditions=False)

    xT = nc.dram_tensor("xT", [D, L], F32, kind="ExternalInput").ap()
    out_d = nc.dram_tensor("out", [D, L], F32, kind="ExternalOutput").ap()
    dbg = {i: nc.dram_tensor(f"dbg{i}", [D, TC], F32, kind="ExternalOutput").ap()
           for i in range(4)}
    dbg_st = nc.dram_tensor("dbgst", [128, NE * NST], F32, kind="ExternalOutput").ap()
    dbg_h2 = nc.dram_tensor("dbgh2", [D, TC], F32, kind="ExternalOutput").ap()
    dbg_g = nc.dram_tensor("dbgg", [128, 48], F32, kind="ExternalOutput").ap()
    dbg_q = nc.dram_tensor("dbgq", [128, 16], F32, kind="ExternalOutput").ap()

    ins = {}

    def din(name, shape, dt=F32):
        ins[name] = nc.dram_tensor(name, shape, dt, kind="ExternalInput").ap()
        return ins[name]

    for li in range(2):
        din(f"inw{li}", [32, 128, 1024])
        din(f"outw{li}", [8, 128, 2048])
        din(f"xpw{li}", [128, NE * 96])
        din(f"dtw{li}", [R, ED])
        din(f"cw{li}", [128, NE * KCV])
        din(f"cb{li}", [128, NE])
        din(f"dtb{li}", [128, NE])
        din(f"alog{li}", [128, NE * NST])
        din(f"dp{li}", [128, NE])
        din(f"nw{li}", [128, NDC])
    din("wg", [32, 128, 1024])
    din("wu", [32, 128, 1024])
    din("wd", [32, 128, 1024])
    din("wmod", [D, 6 * D])
    din("ccol", [128, NDC])
    din("eye", [128, 128])
    din("eyebf", [128, 128], BF16)
    din("onehbf", [32, 32 * 128], BF16)

    stg = {}
    for li in range(2):
        stg[f"inw{li}"] = nc.dram_tensor(f"g_inw{li}", [32, 128, 1024], BF16, kind="Internal").ap()
        stg[f"outw{li}"] = nc.dram_tensor(f"g_outw{li}", [8, 128, 2048], BF16, kind="Internal").ap()
    stg["wg"] = nc.dram_tensor("g_wg", [32, 128, 1024], BF16, kind="Internal").ap()
    stg["wu"] = nc.dram_tensor("g_wu", [32, 128, 1024], BF16, kind="Internal").ap()
    stg["wd"] = nc.dram_tensor("g_wd", [32, 128, 1024], BF16, kind="Internal").ap()
    rowb = nc.dram_tensor("rowb", [16, 128], F32, kind="Internal").ap()

    ops = []
    _selfw = {"v": [], "a": [], "t": [], "s": [], "p": []}
    _last_entry = {"v": "op", "a": "op", "t": "op", "s": "op", "p": "op"}
    _label = ["init"]

    def _sw_check(key):
        for sem_ in _selfw[key]:
            if sem_.n - sem_.last_sw >= 128:
                val = max(sem_.n - 64, 0)
                sem_.last_sw = sem_.n

                def f(eng, sem_=sem_, val=val):
                    eng.wait_ge(sem_.h, val)
                ops.append((key, f, _label[0]))

    def V(fn): _sw_check("v"); _last_entry["v"] = "op"; ops.append(("v", fn, _label[0]))
    def P(fn): _sw_check("p"); _last_entry["p"] = "op"; ops.append(("p", fn, _label[0]))
    def A(fn): _sw_check("a"); _last_entry["a"] = "op"; ops.append(("a", fn, _label[0]))
    def T(fn): _sw_check("t"); _last_entry["t"] = "op"; ops.append(("t", fn, _label[0]))
    def SY(fn): _sw_check("s"); _last_entry["s"] = "op"; ops.append(("s", fn, _label[0]))

    def LBL(name): _label[0] = name

    with ExitStack() as ctx:
        e = ctx.enter_context

        def sbuf(name, shape, dt=F32):
            return e(nc.sbuf_tensor(name, shape, dt))

        h_s = [sbuf(f"h{m}", [128, TC]) for m in range(NDC)]
        st_state = [sbuf(f"stst{li}", [128, NE * NST]) for li in range(2)]
        tails = [sbuf(f"tails{li}", [128, NE * 3], BF16) for li in range(2)]
        xn_bf = [sbuf(f"xn{m}", [128, TC], BF16) for m in range(NDC)]
        xld = [sbuf(f"xld{i}", [128, TC]) for i in range(2)]
        dlt2 = [sbuf(f"dlt{i}", [128, TC], BF16) for i in range(2)]
        vv2 = [sbuf(f"vv{i}", [128, TC], BF16) for i in range(2)]
        t32a = sbuf("t32a", [128, TC])
        t32b = sbuf("t32b", [128, TC])
        t32c = sbuf("t32c", [128, TC])
        ti32 = sbuf("ti32", [128, TC], I32)
        tb16a = sbuf("tb16a", [128, TC], BF16)
        dr_bf = sbuf("dr_bf", [64, TC], BF16)
        bc_bf = sbuf("bc_bf", [32, TC], BF16)
        slabA = [sbuf(f"slabA{i}", [128, 1024], BF16) for i in range(2)]
        slabU = [sbuf(f"slabU{i}", [128, 1024], BF16) for i in range(2)]
        slabB = [sbuf(f"slabB{i}", [128, 2048], BF16) for i in range(2)]
        dtw_sb = [sbuf(f"dtwsb{li}", [64, ED], BF16) for li in range(2)]
        xpw_sb = [sbuf(f"xpwsb{li}", [128, NE * 96], BF16) for li in range(2)]
        cw_sb = [sbuf(f"cwsb{li}", [128, NE * KCV]) for li in range(2)]
        cb_sb = [sbuf(f"cbsb{li}", [128, NE]) for li in range(2)]
        dtb_sb = [sbuf(f"dtbsb{li}", [128, NE]) for li in range(2)]
        acol_sb = [sbuf(f"acolsb{li}", [128, NE * NST]) for li in range(2)]
        dp_sb = [sbuf(f"dpsb{li}", [128, NE]) for li in range(2)]
        nw_sb = [sbuf(f"nwsb{li}", [128, NDC]) for li in range(2)]
        gates = sbuf("gates", [128, 48])
        gtmp = sbuf("gtmp", [128, 8])
        gtmp2 = sbuf("gtmp2", [128, 8])
        ccol_sb = sbuf("ccolsb", [128, NDC])
        csil = sbuf("csil", [128, NDC])
        eye_f = sbuf("eye_f", [128, 128])
        eye_bf = sbuf("eye_bf", [128, 128], BF16)
        oneh_bf = sbuf("onehbf_sb", [32, 32 * 128], BF16)
        ones1 = sbuf("ones1", [1, 128], BF16)
        onesr = sbuf("onesr", [1, 128])
        onesf = sbuf("onesf", [128, 1])
        epscol = sbuf("epscol", [128, 1])
        c_m1 = sbuf("c_m1", [128, 1])
        c_mh = sbuf("c_mh", [128, 1])
        rows = [sbuf(f"row{i}", [1, TC]) for i in range(4)]
        rows_bf = [sbuf(f"rowbf{i}", [1, TC], BF16) for i in range(2)]
        mcols = sbuf("mcols", [128, 16])
        scl16 = sbuf("scl16", [16, 128])
        wqs = sbuf("wqs", [128, 4])
        dcol = sbuf("dcol", [128, 1])

        AR_BYTES = 101 * 1024
        arena = sbuf("arena", [128, AR_BYTES], mybir.dt.uint8)

        def carve(off, cols, dt):
            nb = cols * mybir.dt.size(dt)
            assert off % 32 == 0 and off + nb <= AR_BYTES, (off, nb)
            return arena[:, off:off + nb].bitcast(dt)

        off = 0
        u_bf = []
        zs_bf = []
        ym_bf = []
        for _ in range(NE):
            u_bf.append(carve(off, TC, BF16)); off += TC * 2
        for _ in range(NE):
            zs_bf.append(carve(off, TC, BF16)); off += TC * 2
        for _ in range(NE):
            ym_bf.append(carve(off, TC, BF16)); off += TC * 2
        Bpk = carve(off, NST * TC, BF16); off += NST * TC * 2
        Cpk = carve(off, NST * TC, BF16); off += NST * TC * 2
        Bbc = [Bpk[:, n * TC:(n + 1) * TC] for n in range(NST)]
        Cbc = [Cpk[:, n * TC:(n + 1) * TC] for n in range(NST)]
        PK = 4
        dApk = []
        dbxpk = []
        for _ in range(2):
            dApk.append(carve(off, PK * TC, BF16)); off += PK * TC * 2
        for _ in range(2):
            dbxpk.append(carve(off, PK * TC, BF16)); off += PK * TC * 2
        xi_buf = []
        for _ in range(2):
            xi_buf.append(carve(off, TC + 16, BF16)); off += (TC + 16) * 2
        assert off <= AR_BYTES, off

        off = 0
        hffn = []
        for _ in range(NIC):
            hffn.append(carve(off, TC, F32)); off += TC * 4
        hq_bf = [carve(NIC * TC * 4, TC, BF16)]
        for m in range(1, NIC):
            hq_bf.append(carve((m - 1) * TC * 4, TC, BF16))
        off += TC * 2
        xq_bf = []
        for _ in range(NDC):
            xq_bf.append(carve(off, TC, BF16)); off += TC * 2
        maxt = carve(off, TC, F32); off += TC * 4
        maxn = carve(off, TC, F32); off += TC * 4
        rs_bf = carve(off, TC, BF16); off += TC * 2
        assert off <= AR_BYTES, off

        off = 0
        wf32 = []
        for _ in range(2):
            wf32.append(carve(off, 4096, F32)); off += 4096 * 4
        wb16 = []
        for _ in range(4):
            wb16.append(carve(off, 4096, BF16)); off += 4096 * 2
        wo16 = []
        for _ in range(2):
            wo16.append(carve(off, 1024, BF16)); off += 1024 * 2
        assert off <= AR_BYTES, off

        pbank = [e(nc.psum_tensor(f"pb{i}", [128, 512], F32)) for i in range(4)]
        pdr = e(nc.psum_tensor("pdr", [64, 512], F32))
        pbc = e(nc.psum_tensor("pbc", [32, 512], F32))
        pcast = e(nc.psum_tensor("pcast", [128, 512], F32))
        pmisc = e(nc.psum_tensor("pmisc", [128, 512], F32))

        sws = [S(e(nc.semaphore(f"sw{i}"))) for i in range(8)]
        ssts = [S(e(nc.semaphore(f"sst{i}"))) for i in range(4)]
        dma_ctr = [0, 0]
        spe = S(e(nc.semaphore("spe")))
        sd = S(e(nc.semaphore("sd")))
        sa = S(e(nc.semaphore("sa")))
        sdA = S(e(nc.semaphore("sdA")))
        ssc = S(e(nc.semaphore("ssc")))
        sgp = [S(e(nc.semaphore(f"sgp{i}"))) for i in range(2)]
        spp = S(e(nc.semaphore("spp")))
        sgd = S(e(nc.semaphore("sgd")))

        _selfw["v"] += [sd, ssc]
        _selfw["a"] += [sa, sdA]
        _selfw["t"] += [spe, sgd]
        _selfw["p"] += [spp]
        block = e(nc.Block())

        def _dma(sem_, out_ap, in_ap, gate):
            prev = sem_.n
            v = sem_.inc(16)

            def f(eng, sem_=sem_, prev=prev, out_ap=out_ap, in_ap=in_ap,
                  gate=gate):
                if prev > 0:
                    eng.wait_ge(sem_.h, prev)
                if gate is not None:
                    eng.wait_ge(gate[0].h, gate[1])
                eng.dma_start(out=out_ap, in_=in_ap).then_inc(sem_.h, 16)
            SY(f)
            return (sem_, v)

        def dma_in(out_ap, in_ap, gate=None):
            sem_ = sws[dma_ctr[0] % 8]
            dma_ctr[0] += 1
            return _dma(sem_, out_ap, in_ap, gate)

        def dma_out(out_ap, in_ap, gate=None):
            sem_ = ssts[dma_ctr[1] % 4]
            dma_ctr[1] += 1
            return _dma(sem_, out_ap, in_ap, gate)

        def wtv(key, ret):
            wt(key, ret[0], ret[1])

        gp_ctr = [0]

        def dma_gp(out_ap, in_ap, gate=None):
            sem_ = sgp[gp_ctr[0] % 2]
            gp_ctr[0] += 1
            prev = sem_.n
            v = sem_.inc(16)

            def f(eng, sem_=sem_, prev=prev, out_ap=out_ap, in_ap=in_ap,
                  gate=gate):
                if prev > 0:
                    eng.wait_ge(sem_.h, prev)
                if gate is not None:
                    eng.wait_ge(gate[0].h, gate[1])
                eng.dma_start(out=out_ap, in_=in_ap).then_inc(sem_.h, 16)
            P(f)
            return (sem_, v)

        def all_vals(sems):
            return [(s_, s_.n) for s_ in sems]

        def wt(key, sem_, val):
            def f(eng, sem_=sem_, val=val):
                eng.wait_ge(sem_.h, val)
            ops.append((key, f, _label[0]))

        def inc(key, sem_, hard=True):
            if hard and key in ("v", "a"):
                v = sem_.inc()

                def f(eng, sem_=sem_):
                    eng.drain().then_inc(sem_.h, 1)
                ops.append((key, f, _label[0]))
                _last_entry[key] = "op"
                return v
            if _last_entry[key] == "inc":
                def fc(eng):
                    if key == "a":
                        eng.copy(out=dcol[:], in_=onesf[:])
                    elif key == "t":
                        raise AssertionError("PE double-inc unexpected")
                    else:
                        eng.tensor_copy(out=dcol[:], in_=onesf[:])
                ops.append((key, fc, _label[0]))
            _last_entry[key] = "inc"
            v = sem_.inc()

            def f(eng, sem_=sem_):
                eng.last.then_inc(sem_.h, 1)
            ops.append((key, f, _label[0]))
            return v

        def T_inc(fn):
            v = spe.inc()

            def g(eng, fn=fn):
                mi = fn(eng)
                mi.then_inc(spe.h, 1)
            T(g)
            return v

        bank_free = {}

        def pe_wait_bank(b):
            if b in bank_free:
                s_, v_ = bank_free[b]
                wt("t", s_, v_)
                del bank_free[b]

        LBL("ph0")
        dma_in(ccol_sb[:], ins["ccol"])
        dma_in(eye_f[:], ins["eye"])
        dma_in(eye_bf[:], ins["eyebf"])
        dma_in(oneh_bf[:], ins["onehbf"])
        for li in range(2):
            dma_in(cw_sb[li][:], ins[f"cw{li}"])
            dma_in(cb_sb[li][:], ins[f"cb{li}"])
            dma_in(dtb_sb[li][:], ins[f"dtb{li}"])
            dma_in(acol_sb[li][:], ins[f"alog{li}"])
            dma_in(dp_sb[li][:], ins[f"dp{li}"])
            dma_in(nw_sb[li][:], ins[f"nw{li}"])
        wconst = all_vals(sws)

        for s_, v_ in wconst:
            wt("v", s_, v_)

        def f(eng):
            eng.memset(ones1[:], 1.0)
            eng.memset(onesr[:], 1.0)
            eng.memset(onesf[:], 1.0)
            eng.memset(epscol[:], EPS)
            eng.memset(c_m1[:], -1.0)
            eng.memset(c_mh[:], -0.5)
            eng.memset(st_state[0][:], 0.0)
            eng.memset(st_state[1][:], 0.0)
            eng.memset(tails[0][:], 0.0)
            eng.memset(tails[1][:], 0.0)
        V(f)
        v_vconst = inc("v", sd)

        for s_, v_ in wconst:
            wt("a", s_, v_)

        def f(eng):
            for li in range(2):
                eng.activation(out=acol_sb[li][:], in_=acol_sb[li][:], func=AF.Exp)
        A(f)
        v_acol_a = inc("a", sa)
        wt("v", sa, v_acol_a)

        def f(eng):
            for li in range(2):
                eng.tensor_scalar(out=acol_sb[li][:], in0=acol_sb[li][:],
                                  scalar1=-1.0, scalar2=None, op0=ALU.mult)
        V(f)
        v_acol = inc("v", sd)

        last_small = None
        for li in range(2):
            vw = dma_in(wf32[0][0:64, 0:2048], ins[f"dtw{li}"],
                        gate=(sd, last_small) if last_small else None)
            wtv("v", vw)

            def f(eng, li=li):
                eng.tensor_copy(out=dtw_sb[li][:], in_=wf32[0][0:64, 0:2048])
            V(f)
            inc("v", sd)
            vw = dma_in(wf32[1][:, 0:NE * 96], ins[f"xpw{li}"])
            wtv("v", vw)

            def f(eng, li=li):
                eng.tensor_copy(out=xpw_sb[li][:], in_=wf32[1][:, 0:NE * 96])
            V(f)
            last_small = inc("v", sd)
        v_smallw = last_small

        LBL("mod")
        for s_, v_ in wconst:
            wt("a", s_, v_)

        def f(eng):
            eng.activation(out=csil[:], in_=ccol_sb[:], func=AF.Silu)
        A(f)
        v_csil = inc("a", sa)

        prev_vpe = None
        for j in range(16):
            gate = (spe, prev_vpe) if prev_vpe is not None else (sd, v_smallw)
            vw = dma_in(
                wf32[j % 2][:, 0:3072],
                ins["wmod"][:, j * 384:(j + 1) * 384].rearrange(
                    "(k p) c -> p k c", p=128),
                gate=gate)
            wtv("t", vw)
            if j == 0:
                wt("t", sa, v_csil)

            def f(eng, j=j):
                mi = None
                for mm in range(3):
                    mc = j * 3 + mm
                    for k in range(NDC):
                        mi = eng.matmul(
                            out=pmisc[:, mc:mc + 1],
                            lhsT=wf32[j % 2][:, k * 384 + mm * 128:
                                             k * 384 + (mm + 1) * 128],
                            rhs=csil[:, k:k + 1],
                            start=(k == 0), stop=(k == NDC - 1))
                return mi
            prev_vpe = T_inc(f)
        v_mod_pe = prev_vpe

        wt("v", spe, v_mod_pe)

        def f(eng):
            eng.tensor_copy(out=gates[:], in_=pmisc[:, 0:48])
            eng.tensor_scalar(out=gates[:, 8:16], in0=gates[:, 8:16],
                              scalar1=1.0, scalar2=None, op0=ALU.add)
            eng.tensor_scalar(out=gates[:, 32:40], in0=gates[:, 32:40],
                              scalar1=1.0, scalar2=None, op0=ALU.add)
        V(f)
        v_gcp = inc("v", sd)
        for c0 in (16, 40):
            def f(eng, c0=c0):
                g = gates[:, c0:c0 + 8]
                sg = t32a[:, 0:8]
                mg = t32b[:, 0:8]
                eng.tensor_scalar(out=sg, in0=g, scalar1=0.0, scalar2=None,
                                  op0=ALU.is_ge)
                eng.drain()
                eng.tensor_scalar(out=sg, in0=sg, scalar1=2.0,
                                  scalar2=-1.0, op0=ALU.mult, op1=ALU.add)
                eng.drain()
                eng.tensor_tensor(out=mg, in0=g, in1=sg, op=ALU.mult)
                eng.drain()
                eng.tensor_scalar(out=mg, in0=mg, scalar1=0.25,
                                  scalar2=None, op0=ALU.max)
                eng.drain()
                eng.tensor_tensor(out=g, in0=mg, in1=sg, op=ALU.mult)
            V(f)
            v_gcp = inc("v", sd)

        v_gates = v_gcp


        LBL("stageW")
        def stage_plain(src, dst, nslab, fcols):
            last_store = [None, None]
            for s_i in range(nslab):
                b = s_i % 2
                gate = last_store[b] if last_store[b] is not None else None
                vw = dma_in(wf32[b][:, 0:fcols], src[s_i], gate=gate)
                wtv("v", vw)

                def f(eng, b=b, fcols=fcols):
                    eng.tensor_copy(out=wb16[b][:, 0:fcols], in_=wf32[b][:, 0:fcols])
                V(f)
                vd = inc("v", sd)
                last_store[b] = dma_out(dst[s_i], wb16[b][:, 0:fcols], gate=(sd, vd))
            return last_store

        plain_vals = []
        for li in range(2):
            plain_vals.append(dma_gp(stg[f"inw{li}"], ins[f"inw{li}"]))
            plain_vals.append(dma_gp(stg[f"outw{li}"], ins[f"outw{li}"]))

        def quant_stage(src, dst, nslab, fcols, rawval):
            def f(eng):
                eng.memset(wqs[:, 3:4], 0.0)
            V(f)
            last_vd = None
            vds = [None] * 4
            for s_i in range(nslab):
                b = s_i % 4
                gate = (sd, vds[b]) if vds[b] is not None else None
                if s_i < 4:
                    gate = rawval
                vw = dma_in(wb16[b][:, 0:fcols], src[s_i], gate=gate)
                wtv("v", vw)

                def f(eng, b=b, fcols=fcols):
                    eng.tensor_reduce(out=wqs[:, 2:3], in_=wb16[b][:, 0:fcols],
                                      axis=AXL.X, op=ALU.add,
                                      apply_absolute_value=True)
                    eng.drain()
                    eng.tensor_tensor(out=wqs[:, 3:4], in0=wqs[:, 3:4],
                                      in1=wqs[:, 2:3], op=ALU.add)
                V(f)
                last_vd = inc("v", sd)
                vds[b] = last_vd
            wt("t", sd, last_vd)
            pe_wait_bank("pmisc")

            def f(eng):
                return eng.matmul(out=pmisc[0:1, 0:1], lhsT=wqs[:, 3:4],
                                  rhs=onesf[:], start=True, stop=True)
            vpe = T_inc(f)
            wt("v", spe, vpe)

            def f(eng):
                eng.tensor_copy(out=wqs[0:1, 2:3], in_=pmisc[0:1, 0:1])
            V(f)
            vd2 = inc("v", sd)
            wt("t", sd, vd2)

            def f(eng):
                return eng.matmul(out=pmisc[:, 4:5], lhsT=onesr[:],
                                  rhs=wqs[0:1, 2:3], start=True, stop=True)
            vpe2 = T_inc(f)
            nelem = float(nslab * 128 * fcols)
            wt("v", spe, vpe2)

            def f(eng, nelem=nelem):
                eng.tensor_scalar(out=wqs[:, 2:3], in0=pmisc[:, 4:5],
                                  scalar1=1.0 / nelem, scalar2=EPS,
                                  op0=ALU.mult, op1=ALU.max)
                eng.drain()
                eng.reciprocal(out=wqs[:, 0:1], in_=wqs[:, 2:3])
                eng.drain()
                eng.tensor_copy(out=wqs[:, 1:2], in_=wqs[:, 2:3])
            V(f)
            v_scal = inc("v", sd)
            bank_free["pmisc"] = (sd, v_scal)
            last_store = [None, None, None, None]
            for s_i in range(nslab):
                b = s_i % 4
                gate = last_store[b] if last_store[b] is not None else None
                vw = dma_in(wb16[b][:, 0:fcols], src[s_i], gate=gate)
                wtv("v", vw)
                if s_i == 0:
                    wt("v", sd, v_scal)
                if last_store[(s_i) % 2 + 2] is not None and False:
                    pass
                if s_i >= 2 and last_store[(s_i - 2) % 4] is not None:
                    wtv("v", last_store[(s_i - 2) % 4])

                def f(eng, b=b, fcols=fcols):
                    for p0 in range(0, fcols, TC):
                        wv = wb16[b][:, p0:p0 + TC]
                        eng.tensor_scalar(out=t32a[:], in0=wv, scalar1=wqs[:, 0:1],
                                          scalar2=None, op0=ALU.mult)
                        eng.tensor_copy(out=ti32[:], in_=t32a[:])
                        eng.tensor_tensor(out=t32b[:], in0=t32a[:], in1=ti32[:],
                                          op=ALU.subtract)
                        eng.tensor_scalar(out=t32a[:], in0=t32b[:], scalar1=0.5,
                                          scalar2=None, op0=ALU.is_ge)
                        eng.tensor_scalar(out=t32c[:], in0=t32b[:], scalar1=-0.5,
                                          scalar2=None, op0=ALU.is_le)
                        eng.tensor_tensor(out=t32b[:], in0=ti32[:], in1=t32a[:],
                                          op=ALU.add)
                        eng.tensor_tensor(out=t32b[:], in0=t32b[:], in1=t32c[:],
                                          op=ALU.subtract)
                        eng.tensor_scalar(out=t32b[:], in0=t32b[:], scalar1=1.0,
                                          scalar2=-1.0, op0=ALU.min, op1=ALU.max)
                        eng.tensor_scalar(out=wo16[b % 2][:, p0:p0 + TC],
                                          in0=t32b[:],
                                          scalar1=wqs[:, 1:2], scalar2=None,
                                          op0=ALU.mult)
                V(f)
                vd = inc("v", sd)
                last_store[b] = dma_out(dst[s_i], wo16[b % 2][:, 0:fcols],
                                        gate=(sd, vd))

        raw_bf = {}
        raw_vals = {}
        for wname in ("wg", "wu", "wd"):
            raw_bf[wname] = nc.dram_tensor(f"r_{wname}", [32, 128, 1024], BF16,
                                           kind="Internal").ap()
            raw_vals[wname] = dma_gp(raw_bf[wname], ins[wname])
        quant_stage(raw_bf["wg"], stg["wg"], 32, 1024, raw_vals["wg"])
        quant_stage(raw_bf["wu"], stg["wu"], 32, 1024, raw_vals["wu"])
        quant_stage(raw_bf["wd"], stg["wd"], 32, 1024, raw_vals["wd"])
        v_stage_stores = all_vals(ssts)
        v_stage_dved = inc("v", sd)

        def emit_ln_rows(src_tiles, get_x):
            vds = []
            for m in range(NDC):
                for (kk, s_, v_) in get_x(m):
                    wt(kk, s_, v_)

                def f(eng, m=m):
                    ap = src_tiles[m]
                    eng.tensor_tensor(out=t32a[:], in0=ap, in1=ap, op=ALU.mult)
                V(f)
                vd = inc("v", sd)
                vds.append(vd)
                wt("t", sd, vd)
                if m == 0:
                    pe_wait_bank("pmisc")
                    pe_wait_bank("pdr")

                def f(eng, m=m):
                    eng.matmul(out=pmisc[0:1, 0:TC], lhsT=onesf[:],
                               rhs=src_tiles[m], start=(m == 0),
                               stop=(m == NDC - 1))
                    return eng.matmul(out=pdr[0:1, 0:TC], lhsT=onesf[:],
                                      rhs=t32a[:], start=(m == 0),
                                      stop=(m == NDC - 1))
                if m < NDC - 1:
                    T(lambda eng, f=f: f(eng) and None)
                else:
                    vpe = T_inc(f)
            wt("v", spe, vpe)

            def f(eng):
                eng.tensor_scalar(out=rows[1][:], in0=pdr[0:1, 0:TC],
                                  scalar1=1.0 / D, scalar2=None, op0=ALU.mult)
                eng.tensor_scalar(out=rows[0][:], in0=pmisc[0:1, 0:TC],
                                  scalar1=1.0 / D, scalar2=None, op0=ALU.mult)
                eng.tensor_tensor(out=rows[2][:], in0=rows[0][:], in1=rows[0][:],
                                  op=ALU.mult)
                eng.tensor_tensor(out=rows[1][:], in0=rows[1][:], in1=rows[2][:],
                                  op=ALU.subtract)
            V(f)
            vd = inc("v", sd)
            wt("a", sd, vd)

            def f(eng):
                eng.activation(out=rows[2][:], in_=rows[1][:], func=AF.Ln,
                               bias=epscol[0:1, 0:1])
                eng.activation(out=rows[3][:], in_=rows[2][:], func=AF.Exp,
                               scale=c_mh[0:1, 0:1])
            A(f)
            va = inc("a", sa)
            wt("v", sa, va)

            def f(eng):
                eng.tensor_copy(out=rows_bf[0][:], in_=rows[3][:])
                eng.tensor_tensor(out=rows[2][:], in0=rows[0][:], in1=rows[3][:],
                                  op=ALU.mult)
                eng.tensor_scalar(out=rows[2][:], in0=rows[2][:], scalar1=-1.0,
                                  scalar2=None, op0=ALU.mult)
                eng.tensor_copy(out=rows_bf[1][:], in_=rows[2][:])
            V(f)
            vd = inc("v", sd)
            wt("t", sd, vd)
            pe_wait_bank("pcast")

            def f(eng):
                eng.matmul(out=pcast[:, 0:TC], lhsT=ones1[:], rhs=rows_bf[0][:],
                           start=True, stop=True)
                return eng.matmul(out=pmisc[:, 0:TC], lhsT=ones1[:],
                                  rhs=rows_bf[1][:], start=True, stop=True)
            return T_inc(f)

        def dump_hs(dst, gate_val):
            for m in range(NDC):
                dma_out(dst[m * 128:(m + 1) * 128, :], h_s[m][:],
                        gate=(sd, gate_val))

        for c in range(NCH):
            tsl = slice(c * TC, (c + 1) * TC)

            LBL(f"c{c}_ln")
            if c == 0:
                wt("v", sd, v_gates)

            xv = [None] * NDC
            xcons = [None] * NDC

            def emit_x_loads():
                for m in range(NDC):
                    gate = (sd, xcons[m - 2]) if m >= 2 and xcons[m - 2] else None
                    xv[m] = dma_in(xld[m % 2][:], xT[m * 128:(m + 1) * 128, tsl],
                                   gate=gate)

            for m in range(NDC):
                gate = (sd, xcons[m - 2]) if m >= 2 and xcons[m - 2] else None
                xv[m] = dma_in(xld[m % 2][:], xT[m * 128:(m + 1) * 128, tsl],
                               gate=gate)
                wtv("v", xv[m])

                def f(eng, m=m):
                    eng.tensor_copy(out=h_s[m][:], in_=xld[m % 2][:])
                V(f)
                xcons[m] = inc("v", sd)

            v_bc1 = emit_ln_rows([h_s[m][:] for m in range(NDC)],
                                 lambda m: [("v", sd, xcons[m])] if m == 0 else [])
            wt("v", spe, v_bc1)

            def f(eng):
                for m in range(NDC):
                    eng.tensor_tensor(out=t32a[:], in0=h_s[m][:],
                                      in1=pcast[:, 0:TC], op=ALU.mult)
                    eng.tensor_tensor(out=t32a[:], in0=t32a[:], in1=pmisc[:, 0:TC],
                                      op=ALU.add)
                    eng.tensor_scalar(out=h_s[m][:], in0=t32a[:],
                                      scalar1=gates[:, 8 + m:9 + m],
                                      scalar2=gates[:, 0 + m:1 + m],
                                      op0=ALU.mult, op1=ALU.add)
            V(f)
            v_h0 = inc("v", sd)
            bank_free["pmisc"] = (sd, v_h0)
            bank_free["pcast"] = (sd, v_h0)


            for li in range(2):
                LBL(f"c{c}_L{li}_rms")
                for m in range(NDC):
                    def f(eng, m=m):
                        eng.tensor_tensor(out=t32a[:], in0=h_s[m][:],
                                          in1=h_s[m][:], op=ALU.mult)
                    V(f)
                    vd = inc("v", sd)
                    wt("t", sd, vd)
                    if m == 0:
                        pe_wait_bank("pdr")

                    def f(eng, m=m):
                        return eng.matmul(out=pdr[0:1, 0:TC], lhsT=onesf[:],
                                          rhs=t32a[:], start=(m == 0),
                                          stop=(m == NDC - 1))
                    if m < NDC - 1:
                        T(lambda eng, f=f: f(eng) and None)
                    else:
                        vpe = T_inc(f)
                wt("v", spe, vpe)

                def f(eng):
                    eng.tensor_scalar(out=rows[1][:], in0=pdr[0:1, 0:TC],
                                      scalar1=1.0 / D, scalar2=None, op0=ALU.mult)
                V(f)
                vd = inc("v", sd)
                wt("a", sd, vd)

                def f(eng):
                    eng.activation(out=rows[2][:], in_=rows[1][:], func=AF.Ln,
                                   bias=epscol[0:1, 0:1])
                    eng.activation(out=rows[3][:], in_=rows[2][:], func=AF.Exp,
                                   scale=c_mh[0:1, 0:1])
                A(f)
                va = inc("a", sa)
                wt("v", sa, va)

                def f(eng):
                    eng.tensor_copy(out=rows_bf[0][:], in_=rows[3][:])
                V(f)
                vd = inc("v", sd)
                wt("t", sd, vd)
                pe_wait_bank("pcast")

                def f(eng):
                    return eng.matmul(out=pcast[:, 0:TC], lhsT=ones1[:],
                                      rhs=rows_bf[0][:], start=True, stop=True)
                vpe = T_inc(f)
                wt("v", spe, vpe)

                def f(eng, li=li):
                    for m in range(NDC):
                        eng.scalar_tensor_tensor(
                            out=xn_bf[m][:], in0=h_s[m][:],
                            scalar=nw_sb[li][:, m:m + 1], in1=pcast[:, 0:TC],
                            op0=ALU.mult, op1=ALU.mult)
                V(f)
                v_xn = inc("v", sd)
                bank_free["pcast"] = (sd, v_xn)
                bank_free["pdr"] = (sd, v_xn)

                LBL(f"c{c}_L{li}_inproj")
                if c == 0 and li == 0:
                    for s_, v_ in v_stage_stores:
                        wt("v", s_, v_)
                        wt("a", s_, v_)
                pe_done = [None] * 32
                for s_i in range(32):
                    b = s_i % 2
                    pb = s_i % 2
                    gate = (spe, pe_done[s_i - 2]) if s_i >= 2 else None
                    if c == 0 and s_i < 2:
                        gate = plain_vals[2 * li]
                    vw = dma_in(slabA[b][:], stg[f"inw{li}"][s_i], gate=gate)
                    wtv("t", vw)
                    if s_i == 0:
                        wt("t", sd, v_xn)
                    pe_wait_bank(pb)

                    def f(eng, b=b, pb=pb):
                        mi = None
                        for k in range(NDC):
                            mi = eng.matmul(out=pbank[pb][:],
                                            lhsT=slabA[b][:, k * 128:(k + 1) * 128],
                                            rhs=xn_bf[k][:],
                                            start=(k == 0), stop=(k == NDC - 1))
                        return mi
                    vpe = T_inc(f)
                    pe_done[s_i] = vpe
                    if s_i < NE:
                        ec = s_i
                        wt("a", spe, vpe)

                        def f(eng, ec=ec, pb=pb):
                            xb = xi_buf[ec % 2]
                            eng.copy(out=xb[:, 3:3 + TC], in_=pbank[pb][:])
                        A(f)
                        va_e = inc("a", sa)
                        wt("v", sa, va_e)

                        def f(eng, ec=ec, pb=pb, li=li):
                            xb = xi_buf[ec % 2]
                            eng.tensor_copy(out=xb[:, 0:3],
                                            in_=tails[li][:, ec * 3:ec * 3 + 3])
                            eng.tensor_scalar(
                                out=tb16a[:], in0=xb[:, 0:TC],
                                scalar1=cw_sb[li][:, ec * 4:ec * 4 + 1],
                                scalar2=cb_sb[li][:, ec:ec + 1],
                                op0=ALU.mult, op1=ALU.add)
                            for k in range(1, KCV):
                                eng.scalar_tensor_tensor(
                                    out=tb16a[:], in0=xb[:, k:k + TC],
                                    scalar=cw_sb[li][:, ec * 4 + k:ec * 4 + k + 1],
                                    in1=tb16a[:], op0=ALU.mult, op1=ALU.add)
                            eng.tensor_copy(out=tails[li][:, ec * 3:ec * 3 + 3],
                                            in_=xb[:, TC:TC + 3])
                        V(f)
                        vd = inc("v", sd)
                        bank_free[pb] = (sd, vd)
                        wt("a", sd, vd)

                        def f(eng, ec=ec):
                            eng.activation(out=u_bf[ec], in_=tb16a[:], func=AF.Silu)
                        A(f)
                        inc("a", sa)
                    else:
                        ec = s_i - NE
                        wt("a", spe, vpe)

                        def f(eng, ec=ec, pb=pb):
                            eng.activation(out=zs_bf[ec], in_=pbank[pb][:],
                                           func=AF.Silu)
                        A(f)
                        va = inc("a", sa)
                        bank_free[pb] = (sa, va)
                v_u_all = inc("a", sa)

                LBL(f"c{c}_L{li}_xp")
                wt("t", sa, v_u_all)
                pe_wait_bank("pdr")
                pe_wait_bank("pbc")

                def f(eng, li=li):
                    for k in range(NE):
                        base = k * 96
                        eng.matmul(out=pdr[:, 0:TC],
                                   lhsT=xpw_sb[li][:, base:base + 64],
                                   rhs=u_bf[k], start=(k == 0), stop=(k == NE - 1))
                    mi = None
                    for k in range(NE):
                        base = k * 96
                        mi = eng.matmul(out=pbc[:, 0:TC],
                                        lhsT=xpw_sb[li][:, base + 64:base + 96],
                                        rhs=u_bf[k], start=(k == 0),
                                        stop=(k == NE - 1))
                    return mi
                v_xp = T_inc(f)
                wt("v", spe, v_xp)

                def f(eng):
                    eng.tensor_copy(out=dr_bf[:], in_=pdr[:, 0:TC])
                    eng.tensor_copy(out=bc_bf[:], in_=pbc[:, 0:TC])
                V(f)
                v_drbc = inc("v", sd)
                bank_free["pdr"] = (sd, v_drbc)
                bank_free["pbc"] = (sd, v_drbc)

                wt("t", sd, v_drbc)
                for n in range(2 * NST):
                    pb = 2 + (n % 2)
                    pe_wait_bank(pb)

                    def f(eng, n=n, pb=pb):
                        return eng.matmul(out=pbank[pb][:],
                                          lhsT=oneh_bf[:, n * 128:(n + 1) * 128],
                                          rhs=bc_bf[:], start=True, stop=True)
                    vpe = T_inc(f)
                    dstt = Bbc[n] if n < NST else Cbc[n - NST]
                    if n % 2 == 0:
                        wt("v", spe, vpe)

                        def f(eng, dstt=dstt, pb=pb):
                            eng.tensor_copy(out=dstt, in_=pbank[pb][:])
                        V(f)
                        vd = inc("v", sd)
                        bank_free[pb] = (sd, vd)
                    else:
                        wt("a", spe, vpe)

                        def f(eng, dstt=dstt, pb=pb):
                            eng.copy(out=dstt, in_=pbank[pb][:])
                        A(f)
                        va = inc("a", sa)
                        bank_free[pb] = (sa, va)
                v_bc_a = inc("a", sa)
                v_bc_v = inc("v", sd)

                LBL(f"c{c}_L{li}_scan")
                da_guard = [None] * 2
                dbx_guard = [None] * 2
                first_pool = [True]
                dtbanks = [pmisc, pcast]

                def emit_dt(ec2, li=li):
                    bk = "dtb%d" % (ec2 % 2)
                    pe_wait_bank(bk)
                    if ec2 < 2:
                        pe_wait_bank("pmisc" if ec2 % 2 == 0 else "pcast")
                    if ec2 == 0:
                        wt("t", sd, v_drbc)

                    def f(eng, ec2=ec2, li=li):
                        return eng.matmul(
                            out=dtbanks[ec2 % 2][:, 0:TC],
                            lhsT=dtw_sb[li][:, ec2 * 128:(ec2 + 1) * 128],
                            rhs=dr_bf[:], start=True, stop=True)
                    v_dt = T_inc(f)
                    wt("a", spe, v_dt)

                    def f(eng, li=li, ec2=ec2):
                        eng.activation(out=t32a[:], in_=dtbanks[ec2 % 2][:, 0:TC],
                                       func=AF.Abs,
                                       bias=dtb_sb[li][:, ec2:ec2 + 1])
                        eng.activation(out=t32a[:], in_=t32a[:], func=AF.Exp,
                                       scale=c_m1[:, 0:1])
                        eng.activation(out=t32b[:], in_=t32a[:], func=AF.Ln,
                                       bias=onesf[:, 0:1])
                    A(f)
                    va = inc("a", sa)
                    wt("v", sa, va)

                    def f(eng, li=li, ec2=ec2):
                        eng.tensor_scalar(out=t32c[:], in0=dtbanks[ec2 % 2][:, 0:TC],
                                          scalar1=dtb_sb[li][:, ec2:ec2 + 1],
                                          scalar2=0.0, op0=ALU.add, op1=ALU.max)
                        eng.tensor_tensor(out=dlt2[ec2 % 2][:], in0=t32c[:],
                                          in1=t32b[:], op=ALU.add)
                        eng.tensor_tensor(out=vv2[ec2 % 2][:], in0=dlt2[ec2 % 2][:],
                                          in1=u_bf[ec2], op=ALU.mult)
                    V(f)
                    v_dlt = inc("v", sd)
                    bank_free[bk] = (sd, v_dlt)
                    return v_dlt

                v_dlt_arr = [None] * (NE + 1)
                v_dlt_arr[0] = emit_dt(0)
                for ec in range(NE):
                    if ec + 1 < NE:
                        v_dlt_arr[ec + 1] = emit_dt(ec + 1)
                    v_dlt = v_dlt_arr[ec]
                    dlt = dlt2[ec % 2]
                    vv = vv2[ec % 2]
                    wt("p", sd, v_dlt)
                    if first_pool[0]:
                        wt("p", sa, v_bc_a)
                        wt("p", sd, v_bc_v)
                        first_pool[0] = False
                    yb = "yb%d" % (ec % 2)
                    ybank = pbank[2 + ec % 2]

                    for q in range(NST // PK):
                        bidx = q % 2
                        nb = q * PK
                        wt("a", sd, v_dlt)
                        if da_guard[bidx] is not None:
                            wt("a", ssc, da_guard[bidx])

                        def f(eng, li=li, ec=ec, nb=nb, bidx=bidx, dlt=dlt):
                            for j in range(PK):
                                eng.activation(
                                    out=dApk[bidx][:, j * TC:(j + 1) * TC],
                                    in_=dlt[:], func=AF.Exp,
                                    scale=acol_sb[li][:, ec * NST + nb + j:
                                                      ec * NST + nb + j + 1])
                        A(f)
                        v_da = inc("a", sdA, hard=False)
                        if dbx_guard[bidx] is not None:
                            wt("p", spe, dbx_guard[bidx])

                        def f(eng, nb=nb, bidx=bidx, vv=vv):
                            for j in range(PK):
                                eng.tensor_tensor(
                                    out=dbxpk[bidx][:, j * TC:(j + 1) * TC],
                                    in0=vv[:], in1=Bbc[nb + j], op=ALU.mult)
                        P(f)
                        v_pp = inc("p", spp, hard=False)
                        wt("v", sdA, v_da)
                        wt("v", spp, v_pp)
                        sbase = ec * NST + nb

                        def f(eng, li=li, sbase=sbase, bidx=bidx):
                            dstr = dApk[bidx].rearrange("p (j t) -> p j t", t=TC)
                            bstr = dbxpk[bidx].rearrange("p (j t) -> p j t", t=TC)
                            eng.tensor_tensor(
                                out=mcols[:, 8:8 + PK - 1],
                                in0=dstr[:, 1:PK, 0],
                                in1=st_state[li][:, sbase + 1:sbase + PK],
                                op=ALU.mult)
                            eng.drain()
                            eng.tensor_tensor(
                                out=bstr[:, 1:PK, 0], in0=bstr[:, 1:PK, 0],
                                in1=mcols[:, 8:8 + PK - 1], op=ALU.add)
                            eng.drain()
                            eng.memset(dstr[:, 1:PK, 0], 0.0)
                            eng.drain()
                            eng.tensor_tensor_scan(
                                out=dApk[bidx][:], data0=dApk[bidx][:],
                                data1=dbxpk[bidx][:],
                                initial=st_state[li][:, sbase:sbase + 1],
                                op0=ALU.mult, op1=ALU.add)
                            eng.drain()
                            eng.tensor_copy(
                                out=st_state[li][:, sbase:sbase + PK],
                                in_=dstr[:, :, TC - 1])
                            eng.tensor_tensor(
                                out=dbxpk[bidx][:], in0=dApk[bidx][:],
                                in1=Cpk[:, nb * TC:(nb + PK) * TC], op=ALU.mult)
                        V(f)
                        vsc = inc("v", ssc)
                        da_guard[bidx] = vsc
                        v_g = inc("v", sgd)
                        wt("t", sgd, v_g)
                        if q == 0:
                            pe_wait_bank(yb)

                        def f(eng, q=q, bidx=bidx, ybank=ybank):
                            mi = None
                            for s_ in range(PK):
                                mi = eng.matmul(
                                    out=ybank[:], lhsT=eye_bf[:],
                                    rhs=dbxpk[bidx][:, s_ * TC:(s_ + 1) * TC],
                                    start=(q == 0 and s_ == 0),
                                    stop=(q == NST // PK - 1 and s_ == PK - 1))
                            return mi
                        dbx_guard[bidx] = T_inc(f)

                    wt("v", spe, dbx_guard[(NST // PK - 1) % 2])
                    def f(eng, li=li, ec=ec):
                        eng.scalar_tensor_tensor(out=tb16a[:], in0=u_bf[ec],
                                                 scalar=dp_sb[li][:, ec:ec + 1],
                                                 in1=pbank[2 + ec % 2][:],
                                                 op0=ALU.mult, op1=ALU.add)
                        eng.tensor_tensor(out=ym_bf[ec], in0=tb16a[:],
                                          in1=zs_bf[ec], op=ALU.mult)
                    V(f)
                    v_yf = inc("v", sd)
                    bank_free["yb%d" % (ec % 2)] = (sd, v_yf)
                v_ym = inc("v", sd)
                for par in range(2):
                    if ("dtb%d" % par) in bank_free:
                        bank_free["pmisc" if par == 0 else "pcast"] = \
                            bank_free.pop("dtb%d" % par)

                LBL(f"c{c}_L{li}_outproj")
                pe_done = [None] * 8
                for s_i in range(8):
                    b = s_i % 2
                    pb = s_i % 2
                    gate = (spe, pe_done[s_i - 2]) if s_i >= 2 else None
                    if c == 0 and s_i < 2:
                        gate = plain_vals[2 * li + 1]
                    vw = dma_in(slabB[b][:], stg[f"outw{li}"][s_i], gate=gate)
                    wtv("t", vw)
                    if s_i == 0:
                        wt("t", sd, v_ym)
                    pe_wait_bank(pb)

                    def f(eng, b=b, pb=pb):
                        mi = None
                        for k in range(NE):
                            mi = eng.matmul(out=pbank[pb][:],
                                            lhsT=slabB[b][:, k * 128:(k + 1) * 128],
                                            rhs=ym_bf[k],
                                            start=(k == 0), stop=(k == NE - 1))
                        return mi
                    vpe = T_inc(f)
                    pe_done[s_i] = vpe
                    wt("v", spe, vpe)

                    def f(eng, m=s_i, pb=pb):
                        eng.tensor_tensor(out=h_s[m][:], in0=h_s[m][:],
                                          in1=pbank[pb][:], op=ALU.add)
                    V(f)
                    vd = inc("v", sd)
                    bank_free[pb] = (sd, vd)


            LBL(f"c{c}_gate")
            xcons = [None] * NDC
            for m in range(NDC):
                gate = (sd, xcons[m - 2]) if m >= 2 and xcons[m - 2] else None
                xv2 = dma_in(xld[m % 2][:], xT[m * 128:(m + 1) * 128, tsl],
                             gate=gate)
                wtv("v", xv2)

                def f(eng, m=m):
                    eng.scalar_tensor_tensor(out=h_s[m][:], in0=h_s[m][:],
                                             scalar=gates[:, 16 + m:17 + m],
                                             in1=xld[m % 2][:],
                                             op0=ALU.mult, op1=ALU.add)
                V(f)
                xcons[m] = inc("v", sd)


            LBL(f"c{c}_ffnln")
            v_bc2 = emit_ln_rows([h_s[m][:] for m in range(NDC)],
                                 lambda m: [])
            wt("v", spe, v_bc2)

            def f(eng):
                for m in range(NDC):
                    eng.tensor_tensor(out=t32a[:], in0=h_s[m][:],
                                      in1=pcast[:, 0:TC], op=ALU.mult)
                    eng.tensor_tensor(out=t32a[:], in0=t32a[:], in1=pmisc[:, 0:TC],
                                      op=ALU.add)
                    eng.tensor_scalar(out=hffn[m], in0=t32a[:],
                                      scalar1=gates[:, 32 + m:33 + m],
                                      scalar2=gates[:, 24 + m:25 + m],
                                      op0=ALU.mult, op1=ALU.add)
                    if m == 0:
                        eng.tensor_copy(out=maxt, in_=hffn[m])
                        eng.tensor_copy(out=maxn, in_=hffn[m])
                    else:
                        eng.tensor_tensor(out=maxt, in0=maxt, in1=hffn[m],
                                          op=ALU.max)
                        eng.tensor_tensor(out=maxn, in0=maxn, in1=hffn[m],
                                          op=ALU.min)
            V(f)
            v_h2 = inc("v", sd)
            bank_free["pcast"] = (sd, v_h2)
            bank_free["pmisc"] = (sd, v_h2)


            def aq_scales(v_ready):
                def f(eng):
                    eng.tensor_scalar(out=t32a[:], in0=maxn, scalar1=-1.0,
                                      scalar2=None, op0=ALU.mult)
                    eng.tensor_tensor(out=maxt, in0=maxt, in1=t32a[:],
                                      op=ALU.max)
                V(f)
                v_ready = inc("v", sd)
                wt("t", sd, v_ready)
                pe_wait_bank("pmisc")

                def f(eng):
                    mi = None
                    for j in range(4):
                        mi = eng.transpose(out=pmisc[:, j * 128:(j + 1) * 128],
                                           in_=maxt[:, j * 128:(j + 1) * 128],
                                           identity=eye_f[:])
                    return mi
                vpe = T_inc(f)
                wt("v", spe, vpe)

                def f(eng):
                    for j in range(4):
                        eng.tensor_reduce(out=mcols[:, j:j + 1],
                                          in_=pmisc[:, j * 128:(j + 1) * 128],
                                          axis=AXL.X, op=ALU.max)
                    eng.drain()
                    eng.tensor_scalar(out=mcols[:, 4:8], in0=mcols[:, 0:4],
                                      scalar1=EPS, scalar2=None, op0=ALU.max)
                    eng.drain()
                    eng.reciprocal(out=mcols[:, 0:4], in_=mcols[:, 4:8])
                    eng.drain()
                    eng.tensor_scalar(out=mcols[:, 0:4], in0=mcols[:, 0:4],
                                      scalar1=127.0, scalar2=None, op0=ALU.mult)
                    eng.drain()
                    eng.tensor_scalar(out=mcols[:, 4:8], in0=mcols[:, 4:8],
                                      scalar1=1.0 / 127.0, scalar2=None,
                                      op0=ALU.mult)
                V(f)
                vd = inc("v", sd)
                bank_free["pmisc"] = (sd, vd)
                wt("t", sd, vd)
                pe_wait_bank("pmisc")

                def f(eng):
                    return eng.transpose(out=pmisc[0:8, 0:128], in_=mcols[:, 0:8],
                                         identity=eye_f[:])
                vpe = T_inc(f)
                wt("v", spe, vpe)

                def f(eng):
                    eng.tensor_copy(out=scl16[0:8, :], in_=pmisc[0:8, 0:128])
                V(f)
                vd = inc("v", sd)
                bank_free["pmisc"] = (sd, vd)
                vs1 = dma_out(rowb[0:8], scl16[0:8, :], gate=(sd, vd))
                vrow = dma_in(rows[0][:],
                              rowb[0:4].rearrange("a b -> (a b)"),
                              gate=vs1)
                vrow2 = dma_in(rows[1][:],
                               rowb[4:8].rearrange("a b -> (a b)"))
                wtv("v", vrow2)

                def f(eng):
                    eng.tensor_copy(out=rows_bf[0][:], in_=rows[0][:])
                    eng.tensor_copy(out=rows_bf[1][:], in_=rows[1][:])
                V(f)
                vd = inc("v", sd)
                wt("t", sd, vd)
                pe_wait_bank("pcast")
                pe_wait_bank("pmisc")

                def f(eng):
                    eng.matmul(out=pcast[:, 0:TC], lhsT=ones1[:], rhs=rows_bf[0][:],
                               start=True, stop=True)
                    return eng.matmul(out=pmisc[:, 0:TC], lhsT=ones1[:],
                                      rhs=rows_bf[1][:], start=True, stop=True)
                vpe = T_inc(f)
                return vpe

            def aq_apply(eng, srct, dstt):
                eng.tensor_tensor(out=t32a[:], in0=srct,
                                  in1=pcast[:, 0:TC], op=ALU.mult)
                eng.tensor_copy(out=ti32[:], in_=t32a[:])
                eng.tensor_tensor(out=t32b[:], in0=t32a[:], in1=ti32[:],
                                  op=ALU.subtract)
                eng.tensor_scalar(out=t32a[:], in0=t32b[:], scalar1=0.5,
                                  scalar2=None, op0=ALU.is_ge)
                eng.tensor_scalar(out=t32c[:], in0=t32b[:], scalar1=-0.5,
                                  scalar2=None, op0=ALU.is_le)
                eng.tensor_tensor(out=t32b[:], in0=ti32[:], in1=t32a[:],
                                  op=ALU.add)
                eng.tensor_tensor(out=t32b[:], in0=t32b[:], in1=t32c[:],
                                  op=ALU.subtract)
                eng.tensor_tensor(out=dstt, in0=t32b[:],
                                  in1=pmisc[:, 0:TC], op=ALU.mult)

            def act_quant(src_tiles, nt, dst_tiles, v_ready):
                vpe = aq_scales(v_ready)
                wt("v", spe, vpe)

                def f(eng, nt=nt):
                    for m in range(nt):
                        aq_apply(eng, src_tiles[m], dst_tiles[m])
                V(f)
                vq = inc("v", sd)
                bank_free["pcast"] = (sd, vq)
                bank_free["pmisc"] = (sd, vq)
                return vq

            LBL(f"c{c}_xq")
            v_xq = act_quant(hffn, NDC, xq_bf, v_h2)

            LBL(f"c{c}_gu")
            if c == 0:
                for s_, v_ in v_stage_stores:
                    def f(eng, s_=s_, v_=v_):
                        eng.wait_ge(s_.h, v_)
                    SY(f)
            pe_done = [None] * NIC
            rs_guard = None
            for s_i in range(NIC):
                b = s_i % 2
                pbg = s_i % 2
                pbu = 2 + (s_i % 2)
                gate = (spe, pe_done[s_i - 2]) if s_i >= 2 else None
                dma_in(slabA[b][:], stg["wg"][s_i], gate=gate)
                vw = dma_in(slabU[b][:], stg["wu"][s_i])
                wtv("t", vw)
                if s_i == 0:
                    wt("t", sd, v_xq)
                pe_wait_bank(pbg)
                pe_wait_bank(pbu)

                def f(eng, b=b, pbg=pbg, pbu=pbu):
                    for k in range(NDC):
                        eng.matmul(out=pbank[pbg][:],
                                   lhsT=slabA[b][:, k * 128:(k + 1) * 128],
                                   rhs=xq_bf[k], start=(k == 0),
                                   stop=(k == NDC - 1))
                    mi = None
                    for k in range(NDC):
                        mi = eng.matmul(out=pbank[pbu][:],
                                        lhsT=slabU[b][:, k * 128:(k + 1) * 128],
                                        rhs=xq_bf[k], start=(k == 0),
                                        stop=(k == NDC - 1))
                    return mi
                vpe = T_inc(f)
                pe_done[s_i] = vpe
                wt("a", spe, vpe)
                if rs_guard is not None:
                    wt("a", sd, rs_guard)

                def f(eng, pbg=pbg):
                    eng.activation(out=rs_bf, in_=pbank[pbg][:], func=AF.Relu)
                A(f)
                va = inc("a", sa)
                bank_free[pbg] = (sa, va)
                wt("v", sa, va)

                def f(eng, m=s_i, pbu=pbu):
                    eng.tensor_tensor(out=tb16a[:], in0=rs_bf, in1=rs_bf,
                                      op=ALU.mult)
                    eng.tensor_tensor(out=hffn[m], in0=tb16a[:],
                                      in1=pbank[pbu][:], op=ALU.mult)
                    if m == 0:
                        eng.tensor_copy(out=maxt, in_=hffn[m])
                        eng.tensor_copy(out=maxn, in_=hffn[m])
                    else:
                        eng.tensor_tensor(out=maxt, in0=maxt, in1=hffn[m],
                                          op=ALU.max)
                        eng.tensor_tensor(out=maxn, in0=maxn, in1=hffn[m],
                                          op=ALU.min)
                V(f)
                vd = inc("v", sd)
                rs_guard = vd
                bank_free[pbu] = (sd, vd)
            v_hmax = inc("v", sd)

            LBL(f"c{c}_hq")
            v_hqsc = aq_scales(v_hmax)

            LBL(f"c{c}_down")
            hq_ready = [None] * NIC
            out_guard = [None, None]
            for half in range(2):
                pe_done = [None] * NIC
                for k in range(NIC):
                    b = k % 2
                    gate = (spe, pe_done[k - 2]) if k >= 2 else None
                    vw = dma_in(slabU[b][:], stg["wd"][k], gate=gate)
                    if half == 0:
                        if k == 0:
                            wt("v", spe, v_hqsc)

                        def f(eng, k=k):
                            aq_apply(eng, hffn[k], hq_bf[k])
                        V(f)
                        hq_ready[k] = inc("v", sd)
                    wtv("t", vw)
                    wt("t", sd, hq_ready[k])
                    if k == 0:
                        for mh in range(4):
                            pe_wait_bank(mh)

                    def f(eng, k=k, b=b, half=half):
                        mi = None
                        for mh in range(4):
                            m = half * 4 + mh
                            mi = eng.matmul(out=pbank[mh][:],
                                            lhsT=slabU[b][:, m * 128:(m + 1) * 128],
                                            rhs=hq_bf[k], start=(k == 0),
                                            stop=(k == NIC - 1))
                        return mi
                    pe_done[k] = T_inc(f)
                vpe_half = pe_done[NIC - 1]
                for mh in range(4):
                    m = half * 4 + mh
                    wt("v", spe, vpe_half)
                    if out_guard[m % 2] is not None:
                        wtv("v", out_guard[m % 2])

                    def f(eng, m=m, mh=mh):
                        dst = t32a if m % 2 == 0 else t32b
                        eng.scalar_tensor_tensor(out=dst[:], in0=pbank[mh][:],
                                                 scalar=gates[:, 40 + m:41 + m],
                                                 in1=h_s[m][:], op0=ALU.mult,
                                                 op1=ALU.add)
                    V(f)
                    vd = inc("v", sd)
                    bank_free[mh] = (sd, vd)
                    out_guard[m % 2] = dma_out(
                        out_d[m * 128:(m + 1) * 128, tsl],
                        (t32a if m % 2 == 0 else t32b)[:], gate=(sd, vd))
            vq_end = inc("v", sd)
            bank_free["pcast"] = (sd, vq_end)
            bank_free["pmisc"] = (sd, vq_end)

        total_stores = all_vals(ssts)

        def f(eng):
            for s_, v_ in total_stores:
                eng.wait_ge(s_.h, v_)
        SY(f)

        class EngProxy:
            def __init__(self, eng):
                self._e = eng
                self.last = None

            def __getattr__(self, name):
                a = getattr(self._e, name)
                if callable(a):
                    track = not (name.startswith("wait") or name == "sem_inc")

                    def w(*args, _a=a, _t=track, **kw):
                        r = _a(*args, **kw)
                        if _t and r is not None and hasattr(r, "then_inc"):
                            self.last = r
                        return r
                    return w
                return a

        from contextlib import ExitStack as _ES

        def _replay(kind, eng):
            p = EngProxy(eng)
            cur = [None]
            with _ES() as st_:
                for k, fn, lbl in ops:
                    if k != kind:
                        continue
                    if lbl != cur[0]:
                        st_.pop_all().close()
                        st_ = _ES()
                        st_.enter_context(nc.named_scope(lbl))
                        cur[0] = lbl
                    fn(p)
                st_.pop_all().close()

        @block.sync
        def _(eng):
            _replay("s", eng)

        @block.vector
        def _(eng):
            _replay("v", eng)

        @block.scalar
        def _(eng):
            _replay("a", eng)

        @block.tensor
        def _(eng):
            _replay("t", eng)

        @block.gpsimd
        def _(eng):
            _replay("p", eng)

    return nc



def _cols(vec, p=128):
    k = vec.shape[0] // p
    return np.ascontiguousarray(vec.reshape(k, p).T)


def _slab_major(wT, nslab):
    K, M = wT.shape
    nk = K // 128
    ms = M // nslab
    a = wT.reshape(nk, 128, nslab, ms)
    a = a.transpose(2, 1, 0, 3).reshape(nslab, 128, nk * ms)
    return np.ascontiguousarray(a)


def kernel(L0_attn_x, L0_attn_in_proj_w, L0_attn_conv_w, L0_attn_conv_b,
           L0_attn_x_proj_w, L0_attn_dt_w, L0_attn_dt_b, L0_attn_A_log,
           L0_attn_D_p, L0_attn_out_proj_w, L0_attn_norm_w, L0_ffn_w_gate,
           L0_ffn_w_up, L0_ffn_w_down, L0_c, L0_wmod):
    f32 = np.float32
    common = {}
    for li in range(2):
        common[f"inw{li}"] = _slab_major(
            np.ascontiguousarray(np.asarray(L0_attn_in_proj_w)[li].T.astype(f32)), 32)
        common[f"outw{li}"] = _slab_major(
            np.ascontiguousarray(np.asarray(L0_attn_out_proj_w)[li].T.astype(f32)), 8)
        xpwT = np.ascontiguousarray(np.asarray(L0_attn_x_proj_w)[li].T.astype(f32))
        common[f"xpw{li}"] = np.ascontiguousarray(
            xpwT.reshape(16, 128, 96).transpose(1, 0, 2).reshape(128, 16 * 96))
        common[f"dtw{li}"] = np.ascontiguousarray(np.asarray(L0_attn_dt_w)[li].T.astype(f32))
        cw = np.asarray(L0_attn_conv_w)[li, :, 0, :].astype(f32)
        common[f"cw{li}"] = np.ascontiguousarray(
            cw.reshape(16, 128, 4).transpose(1, 0, 2).reshape(128, 64))
        common[f"cb{li}"] = _cols(np.asarray(L0_attn_conv_b)[li].astype(f32))
        common[f"dtb{li}"] = _cols(np.asarray(L0_attn_dt_b)[li].astype(f32))
        al = np.asarray(L0_attn_A_log)[li].astype(f32)
        common[f"alog{li}"] = np.ascontiguousarray(
            al.reshape(16, 128, 16).transpose(1, 0, 2).reshape(128, 256))
        common[f"dp{li}"] = _cols(np.asarray(L0_attn_D_p)[li].astype(f32))
        common[f"nw{li}"] = _cols(np.asarray(L0_attn_norm_w)[li].astype(f32))
    common["wg"] = _slab_major(np.ascontiguousarray(np.asarray(L0_ffn_w_gate).T.astype(f32)), 32)
    common["wu"] = _slab_major(np.ascontiguousarray(np.asarray(L0_ffn_w_up).T.astype(f32)), 32)
    common["wd"] = np.ascontiguousarray(
        np.asarray(L0_ffn_w_down).T.astype(f32).reshape(32, 128, 1024))
    common["wmod"] = np.ascontiguousarray(np.asarray(L0_wmod).astype(f32))
    common["eye"] = np.eye(128, dtype=f32)
    common["eyebf"] = np.eye(128, dtype=ml_dtypes.bfloat16)
    oneh = np.zeros((32, 32 * 128), dtype=ml_dtypes.bfloat16)
    for n in range(32):
        oneh[n, n * 128:(n + 1) * 128] = 1
    common["onehbf"] = oneh

    nc = build_nc()
    in_maps = []
    for b in range(8):
        m = dict(common)
        m["xT"] = np.ascontiguousarray(np.asarray(L0_attn_x)[b].astype(f32).T)
        m["ccol"] = _cols(np.asarray(L0_c)[b].astype(f32))
        in_maps.append(m)
    res = run_bass_kernel_spmd(nc, in_maps, core_ids=list(range(8)))
    outs = [np.asarray(r["out"], dtype=np.float32).T for r in res.results]
    return np.ascontiguousarray(np.stack(outs, axis=0))


# version 17 (final)
# speedup vs baseline: 1.0173x; 1.0173x over previous
import numpy as np
import ml_dtypes
from contextlib import ExitStack

import concourse.bass as bass
import concourse.mybir as mybir
from concourse.bass_utils import run_bass_kernel_spmd

F32 = mybir.dt.float32
BF16 = mybir.dt.bfloat16
I32 = mybir.dt.int32
AF = mybir.ActivationFunctionType
ALU = mybir.AluOpType
AXL = mybir.AxisListType

L = 2048
D = 1024
ED = 2048
NST = 16
KCV = 4
R = 64
IFF = 4096
TC = 512
NCH = L // TC
EPS = 1e-5
NE = ED // 128
NDC = D // 128
NIC = IFF // 128
NGRP = 8


class S:
    def __init__(self, h):
        self.h = h
        self.n = 0
        self.last_sw = 0

    def inc(self, k=1):
        self.n += k
        return self.n


def build_nc():
    nc = bass.Bass("TRN2", target_bir_lowering=False, detect_race_conditions=False)

    xT = nc.dram_tensor("xT", [D, L], F32, kind="ExternalInput").ap()
    out_d = nc.dram_tensor("out", [D, L], F32, kind="ExternalOutput").ap()
    dbg = {i: nc.dram_tensor(f"dbg{i}", [D, TC], F32, kind="ExternalOutput").ap()
           for i in range(4)}
    dbg_st = nc.dram_tensor("dbgst", [128, NE * NST], F32, kind="ExternalOutput").ap()
    dbg_h2 = nc.dram_tensor("dbgh2", [D, TC], F32, kind="ExternalOutput").ap()
    dbg_g = nc.dram_tensor("dbgg", [128, 48], F32, kind="ExternalOutput").ap()
    dbg_q = nc.dram_tensor("dbgq", [128, 16], F32, kind="ExternalOutput").ap()

    ins = {}

    def din(name, shape, dt=F32):
        ins[name] = nc.dram_tensor(name, shape, dt, kind="ExternalInput").ap()
        return ins[name]

    for li in range(2):
        din(f"inw{li}", [32, 128, 1024])
        din(f"outw{li}", [8, 128, 2048])
        din(f"xpw{li}", [128, NE * 96])
        din(f"dtw{li}", [R, ED])
        din(f"cw{li}", [128, NE * KCV])
        din(f"cb{li}", [128, NE])
        din(f"dtb{li}", [128, NE])
        din(f"alog{li}", [128, NE * NST])
        din(f"dp{li}", [128, NE])
        din(f"nw{li}", [128, NDC])
    din("wg", [32, 128, 1024])
    din("wu", [32, 128, 1024])
    din("wd", [32, 128, 1024])
    din("wmod", [D, 6 * D])
    din("ccol", [128, NDC])
    din("eye", [128, 128])
    din("eyebf", [128, 128], BF16)
    din("onehbf", [32, 32 * 128], BF16)

    stg = {}
    for li in range(2):
        stg[f"inw{li}"] = nc.dram_tensor(f"g_inw{li}", [32, 128, 1024], BF16, kind="Internal").ap()
        stg[f"outw{li}"] = nc.dram_tensor(f"g_outw{li}", [8, 128, 2048], BF16, kind="Internal").ap()
    stg["wg"] = nc.dram_tensor("g_wg", [32, 128, 1024], BF16, kind="Internal").ap()
    stg["wgu"] = nc.dram_tensor("g_wgu", [64, 128, 1024], BF16, kind="Internal").ap()
    stg["wu"] = nc.dram_tensor("g_wu", [32, 128, 1024], BF16, kind="Internal").ap()
    stg["wd"] = nc.dram_tensor("g_wd", [32, 128, 1024], BF16, kind="Internal").ap()
    rowb = nc.dram_tensor("rowb", [16, 128], F32, kind="Internal").ap()

    ops = []
    _selfw = {"v": [], "a": [], "t": [], "s": [], "p": []}
    _last_entry = {"v": "op", "a": "op", "t": "op", "s": "op", "p": "op"}
    _label = ["init"]

    def _sw_check(key):
        for sem_ in _selfw[key]:
            if sem_.n - sem_.last_sw >= 128:
                val = max(sem_.n - 64, 0)
                sem_.last_sw = sem_.n

                def f(eng, sem_=sem_, val=val):
                    eng.wait_ge(sem_.h, val)
                ops.append((key, f, _label[0]))

    def V(fn): _sw_check("v"); _last_entry["v"] = "op"; ops.append(("v", fn, _label[0]))
    def P(fn): _sw_check("p"); _last_entry["p"] = "op"; ops.append(("p", fn, _label[0]))
    def A(fn): _sw_check("a"); _last_entry["a"] = "op"; ops.append(("a", fn, _label[0]))
    def T(fn): _sw_check("t"); _last_entry["t"] = "op"; ops.append(("t", fn, _label[0]))
    def SY(fn): _sw_check("s"); _last_entry["s"] = "op"; ops.append(("s", fn, _label[0]))

    def LBL(name): _label[0] = name

    with ExitStack() as ctx:
        e = ctx.enter_context

        def sbuf(name, shape, dt=F32):
            return e(nc.sbuf_tensor(name, shape, dt))

        h_s = [sbuf(f"h{m}", [128, TC]) for m in range(NDC)]
        st_state = [sbuf(f"stst{li}", [128, NE * NST]) for li in range(2)]
        tails = [sbuf(f"tails{li}", [128, NE * 3], BF16) for li in range(2)]
        xn_bf = [sbuf(f"xn{m}", [128, TC], BF16) for m in range(NDC)]
        xld = [sbuf(f"xld{i}", [128, TC]) for i in range(2)]
        dlt2 = [sbuf(f"dlt{i}", [128, TC], BF16) for i in range(2)]
        vv2 = [sbuf(f"vv{i}", [128, TC], BF16) for i in range(2)]
        dA = [sbuf(f"dA{i}", [128, TC], BF16) for i in range(2)]
        dbxs = [sbuf(f"dbx{i}", [128, TC], BF16) for i in range(2)]
        gbufs4 = [sbuf(f"gbf{i}", [128, TC], BF16) for i in range(2)]
        t32a = sbuf("t32a", [128, TC])
        t32b = sbuf("t32b", [128, TC])
        t32c = sbuf("t32c", [128, TC])
        ti32 = sbuf("ti32", [128, TC], I32)
        tb16a = sbuf("tb16a", [128, TC], BF16)
        dr_bf = sbuf("dr_bf", [64, TC], BF16)
        bc_bf = sbuf("bc_bf", [32, TC], BF16)
        slabA = [sbuf(f"slabA{i}", [128, 2048], BF16) for i in range(2)]
        slabB = [sbuf(f"slabB{i}", [128, 2048], BF16) for i in range(2)]
        dtw_sb = [sbuf(f"dtwsb{li}", [64, ED], BF16) for li in range(2)]
        xpw_sb = [sbuf(f"xpwsb{li}", [128, NE * 96], BF16) for li in range(2)]
        cw_sb = [sbuf(f"cwsb{li}", [128, NE * KCV]) for li in range(2)]
        cb_sb = [sbuf(f"cbsb{li}", [128, NE]) for li in range(2)]
        dtb_sb = [sbuf(f"dtbsb{li}", [128, NE]) for li in range(2)]
        acol_sb = [sbuf(f"acolsb{li}", [128, NE * NST]) for li in range(2)]
        dp_sb = [sbuf(f"dpsb{li}", [128, NE]) for li in range(2)]
        nw_sb = [sbuf(f"nwsb{li}", [128, NDC]) for li in range(2)]
        gates = sbuf("gates", [128, 48])
        gtmp = sbuf("gtmp", [128, 8])
        gtmp2 = sbuf("gtmp2", [128, 8])
        ccol_sb = sbuf("ccolsb", [128, NDC])
        csil = sbuf("csil", [128, NDC])
        eye_f = sbuf("eye_f", [128, 128])
        eye_bf = sbuf("eye_bf", [128, 128], BF16)
        oneh_bf = sbuf("onehbf_sb", [32, 32 * 128], BF16)
        ones1 = sbuf("ones1", [1, 128], BF16)
        onesr = sbuf("onesr", [1, 128])
        onesf = sbuf("onesf", [128, 1])
        epscol = sbuf("epscol", [128, 1])
        c_m1 = sbuf("c_m1", [128, 1])
        c_mh = sbuf("c_mh", [128, 1])
        rows = [sbuf(f"row{i}", [1, TC]) for i in range(4)]
        rows_bf = [sbuf(f"rowbf{i}", [1, TC], BF16) for i in range(2)]
        mcols = sbuf("mcols", [128, 16])
        scl16 = sbuf("scl16", [16, 128])
        wqs = sbuf("wqs", [128, 4])
        dcol = sbuf("dcol", [128, 1])

        AR_BYTES = 98 * 1024
        arena = sbuf("arena", [128, AR_BYTES], mybir.dt.uint8)

        def carve(off, cols, dt):
            nb = cols * mybir.dt.size(dt)
            assert off % 32 == 0 and off + nb <= AR_BYTES, (off, nb)
            return arena[:, off:off + nb].bitcast(dt)

        off = 0
        u_bf = []
        zs_bf = []
        ym_bf = []
        for _ in range(NE):
            u_bf.append(carve(off, TC, BF16)); off += TC * 2
        for _ in range(NE):
            zs_bf.append(carve(off, TC, BF16)); off += TC * 2
        for _ in range(NE):
            ym_bf.append(carve(off, TC, BF16)); off += TC * 2
        Bbc = []
        Cbc = []
        for _ in range(NST):
            Bbc.append(carve(off, TC, BF16)); off += TC * 2
        for _ in range(NST):
            Cbc.append(carve(off, TC, BF16)); off += TC * 2
        Hbuf = carve(off, NGRP * TC, BF16); off += NGRP * TC * 2
        xi_buf = []
        for _ in range(2):
            xi_buf.append(carve(off, TC + 16, BF16)); off += (TC + 16) * 2
        assert off <= AR_BYTES, off

        off = 0
        hffn = []
        for _ in range(NIC):
            hffn.append(carve(off, TC, F32)); off += TC * 4
        hq_bf = [carve(NIC * TC * 4, TC, BF16)]
        for m in range(1, NIC):
            hq_bf.append(carve((m - 1) * TC * 4, TC, BF16))
        off += TC * 2
        xq_bf = []
        for _ in range(NDC):
            xq_bf.append(carve(off, TC, BF16)); off += TC * 2
        maxt = carve(off, TC, F32); off += TC * 4
        maxn = carve(off, TC, F32); off += TC * 4
        rs_bf = carve(off, TC, BF16); off += TC * 2
        slabD = []
        for _ in range(2):
            slabD.append(carve(off, 4096, BF16)); off += 4096 * 2
        assert off <= AR_BYTES, off

        off = 0
        wf32 = []
        for _ in range(2):
            wf32.append(carve(off, 4096, F32)); off += 4096 * 4
        wb16 = []
        for _ in range(4):
            wb16.append(carve(off, 4096, BF16)); off += 4096 * 2
        wo16 = []
        for _ in range(2):
            wo16.append(carve(off, 1024, BF16)); off += 1024 * 2
        assert off <= AR_BYTES, off

        pbank = [e(nc.psum_tensor(f"pb{i}", [128, 512], F32)) for i in range(4)]
        pdr = e(nc.psum_tensor("pdr", [64, 512], F32))
        pbc = e(nc.psum_tensor("pbc", [32, 512], F32))
        pcast = e(nc.psum_tensor("pcast", [128, 512], F32))
        pmisc = e(nc.psum_tensor("pmisc", [128, 512], F32))

        sws = [S(e(nc.semaphore(f"sw{i}"))) for i in range(8)]
        ssts = [S(e(nc.semaphore(f"sst{i}"))) for i in range(4)]
        dma_ctr = [0, 0]
        spe = S(e(nc.semaphore("spe")))
        sd = S(e(nc.semaphore("sd")))
        sa = S(e(nc.semaphore("sa")))
        sdA = S(e(nc.semaphore("sdA")))
        ssc = S(e(nc.semaphore("ssc")))
        sgp = [S(e(nc.semaphore(f"sgp{i}"))) for i in range(2)]
        spp = S(e(nc.semaphore("spp")))
        sgd = S(e(nc.semaphore("sgd")))

        _selfw["v"] += [sd, ssc]
        _selfw["a"] += [sa, sdA]
        _selfw["t"] += [spe, sgd]
        _selfw["p"] += [spp]
        block = e(nc.Block())

        def _dma(sem_, out_ap, in_ap, gate):
            prev = sem_.n
            v = sem_.inc(16)

            def f(eng, sem_=sem_, prev=prev, out_ap=out_ap, in_ap=in_ap,
                  gate=gate):
                if prev > 0:
                    eng.wait_ge(sem_.h, prev)
                if gate is not None:
                    eng.wait_ge(gate[0].h, gate[1])
                eng.dma_start(out=out_ap, in_=in_ap).then_inc(sem_.h, 16)
            SY(f)
            return (sem_, v)

        def dma_in(out_ap, in_ap, gate=None):
            sem_ = sws[dma_ctr[0] % 8]
            dma_ctr[0] += 1
            return _dma(sem_, out_ap, in_ap, gate)

        def dma_out(out_ap, in_ap, gate=None):
            sem_ = ssts[dma_ctr[1] % 4]
            dma_ctr[1] += 1
            return _dma(sem_, out_ap, in_ap, gate)

        def wtv(key, ret):
            wt(key, ret[0], ret[1])

        gp_ctr = [0]

        def dma_gp(out_ap, in_ap, gate=None):
            sem_ = sgp[gp_ctr[0] % 2]
            gp_ctr[0] += 1
            prev = sem_.n
            v = sem_.inc(16)

            def f(eng, sem_=sem_, prev=prev, out_ap=out_ap, in_ap=in_ap,
                  gate=gate):
                if prev > 0:
                    eng.wait_ge(sem_.h, prev)
                if gate is not None:
                    eng.wait_ge(gate[0].h, gate[1])
                eng.dma_start(out=out_ap, in_=in_ap).then_inc(sem_.h, 16)
            P(f)
            return (sem_, v)

        def all_vals(sems):
            return [(s_, s_.n) for s_ in sems]

        def wt(key, sem_, val):
            def f(eng, sem_=sem_, val=val):
                eng.wait_ge(sem_.h, val)
            ops.append((key, f, _label[0]))

        def inc(key, sem_, hard=True):
            if hard and key in ("v", "a"):
                v = sem_.inc()

                def f(eng, sem_=sem_):
                    eng.drain().then_inc(sem_.h, 1)
                ops.append((key, f, _label[0]))
                _last_entry[key] = "op"
                return v
            if _last_entry[key] == "inc":
                def fc(eng):
                    if key == "a":
                        eng.copy(out=dcol[:], in_=onesf[:])
                    elif key == "t":
                        raise AssertionError("PE double-inc unexpected")
                    else:
                        eng.tensor_copy(out=dcol[:], in_=onesf[:])
                ops.append((key, fc, _label[0]))
            _last_entry[key] = "inc"
            v = sem_.inc()

            def f(eng, sem_=sem_):
                eng.last.then_inc(sem_.h, 1)
            ops.append((key, f, _label[0]))
            return v

        def T_inc(fn):
            v = spe.inc()

            def g(eng, fn=fn):
                mi = fn(eng)
                mi.then_inc(spe.h, 1)
            T(g)
            return v

        bank_free = {}

        def pe_wait_bank(b):
            if b in bank_free:
                s_, v_ = bank_free[b]
                wt("t", s_, v_)
                del bank_free[b]

        LBL("ph0")
        dma_in(ccol_sb[:], ins["ccol"])
        dma_in(eye_f[:], ins["eye"])
        dma_in(eye_bf[:], ins["eyebf"])
        dma_in(oneh_bf[:], ins["onehbf"])
        for li in range(2):
            dma_in(cw_sb[li][:], ins[f"cw{li}"])
            dma_in(cb_sb[li][:], ins[f"cb{li}"])
            dma_in(dtb_sb[li][:], ins[f"dtb{li}"])
            dma_in(acol_sb[li][:], ins[f"alog{li}"])
            dma_in(dp_sb[li][:], ins[f"dp{li}"])
            dma_in(nw_sb[li][:], ins[f"nw{li}"])
        wconst = all_vals(sws)

        for s_, v_ in wconst:
            wt("v", s_, v_)

        def f(eng):
            eng.memset(ones1[:], 1.0)
            eng.memset(onesr[:], 1.0)
            eng.memset(onesf[:], 1.0)
            eng.memset(epscol[:], EPS)
            eng.memset(c_m1[:], -1.0)
            eng.memset(c_mh[:], -0.5)
            eng.memset(st_state[0][:], 0.0)
            eng.memset(st_state[1][:], 0.0)
            eng.memset(tails[0][:], 0.0)
            eng.memset(tails[1][:], 0.0)
        V(f)
        v_vconst = inc("v", sd)

        for s_, v_ in wconst:
            wt("a", s_, v_)

        def f(eng):
            for li in range(2):
                eng.activation(out=acol_sb[li][:], in_=acol_sb[li][:], func=AF.Exp)
        A(f)
        v_acol_a = inc("a", sa)
        wt("v", sa, v_acol_a)

        def f(eng):
            for li in range(2):
                eng.tensor_scalar(out=acol_sb[li][:], in0=acol_sb[li][:],
                                  scalar1=-1.0, scalar2=None, op0=ALU.mult)
        V(f)
        v_acol = inc("v", sd)

        last_small = None
        for li in range(2):
            vw = dma_in(wf32[0][0:64, 0:2048], ins[f"dtw{li}"],
                        gate=(sd, last_small) if last_small else None)
            wtv("v", vw)

            def f(eng, li=li):
                eng.tensor_copy(out=dtw_sb[li][:], in_=wf32[0][0:64, 0:2048])
            V(f)
            inc("v", sd)
            vw = dma_in(wf32[1][:, 0:NE * 96], ins[f"xpw{li}"])
            wtv("v", vw)

            def f(eng, li=li):
                eng.tensor_copy(out=xpw_sb[li][:], in_=wf32[1][:, 0:NE * 96])
            V(f)
            last_small = inc("v", sd)
        v_smallw = last_small

        LBL("mod")
        for s_, v_ in wconst:
            wt("a", s_, v_)

        def f(eng):
            eng.activation(out=csil[:], in_=ccol_sb[:], func=AF.Silu)
        A(f)
        v_csil = inc("a", sa)

        prev_vpe = None
        for j in range(16):
            gate = (spe, prev_vpe) if prev_vpe is not None else (sd, v_smallw)
            vw = dma_in(
                wf32[j % 2][:, 0:3072],
                ins["wmod"][:, j * 384:(j + 1) * 384].rearrange(
                    "(k p) c -> p k c", p=128),
                gate=gate)
            wtv("t", vw)
            if j == 0:
                wt("t", sa, v_csil)

            def f(eng, j=j):
                mi = None
                for mm in range(3):
                    mc = j * 3 + mm
                    for k in range(NDC):
                        mi = eng.matmul(
                            out=pmisc[:, mc:mc + 1],
                            lhsT=wf32[j % 2][:, k * 384 + mm * 128:
                                             k * 384 + (mm + 1) * 128],
                            rhs=csil[:, k:k + 1],
                            start=(k == 0), stop=(k == NDC - 1))
                return mi
            prev_vpe = T_inc(f)
        v_mod_pe = prev_vpe

        wt("v", spe, v_mod_pe)

        def f(eng):
            eng.tensor_copy(out=gates[:], in_=pmisc[:, 0:48])
            eng.tensor_scalar(out=gates[:, 8:16], in0=gates[:, 8:16],
                              scalar1=1.0, scalar2=None, op0=ALU.add)
            eng.tensor_scalar(out=gates[:, 32:40], in0=gates[:, 32:40],
                              scalar1=1.0, scalar2=None, op0=ALU.add)
        V(f)
        v_gcp = inc("v", sd)
        for c0 in (16, 40):
            def f(eng, c0=c0):
                g = gates[:, c0:c0 + 8]
                sg = t32a[:, 0:8]
                mg = t32b[:, 0:8]
                eng.tensor_scalar(out=sg, in0=g, scalar1=0.0, scalar2=None,
                                  op0=ALU.is_ge)
                eng.drain()
                eng.tensor_scalar(out=sg, in0=sg, scalar1=2.0,
                                  scalar2=-1.0, op0=ALU.mult, op1=ALU.add)
                eng.drain()
                eng.tensor_tensor(out=mg, in0=g, in1=sg, op=ALU.mult)
                eng.drain()
                eng.tensor_scalar(out=mg, in0=mg, scalar1=0.25,
                                  scalar2=None, op0=ALU.max)
                eng.drain()
                eng.tensor_tensor(out=g, in0=mg, in1=sg, op=ALU.mult)
            V(f)
            v_gcp = inc("v", sd)

        v_gates = v_gcp


        LBL("stageW")
        def stage_plain(src, dst, nslab, fcols):
            last_store = [None, None]
            for s_i in range(nslab):
                b = s_i % 2
                gate = last_store[b] if last_store[b] is not None else None
                vw = dma_in(wf32[b][:, 0:fcols], src[s_i], gate=gate)
                wtv("v", vw)

                def f(eng, b=b, fcols=fcols):
                    eng.tensor_copy(out=wb16[b][:, 0:fcols], in_=wf32[b][:, 0:fcols])
                V(f)
                vd = inc("v", sd)
                last_store[b] = dma_out(dst[s_i], wb16[b][:, 0:fcols], gate=(sd, vd))
            return last_store

        plain_vals = []
        for li in range(2):
            plain_vals.append(dma_gp(stg[f"inw{li}"], ins[f"inw{li}"]))
            plain_vals.append(dma_gp(stg[f"outw{li}"], ins[f"outw{li}"]))

        def quant_stage(src, dst, nslab, fcols, rawval):
            def f(eng):
                eng.memset(wqs[:, 3:4], 0.0)
            V(f)
            last_vd = None
            vds = [None] * 4
            for s_i in range(nslab):
                b = s_i % 4
                gate = (sd, vds[b]) if vds[b] is not None else None
                if s_i < 4:
                    gate = rawval
                vw = dma_in(wb16[b][:, 0:fcols], src[s_i], gate=gate)
                wtv("v", vw)

                def f(eng, b=b, fcols=fcols):
                    eng.tensor_reduce(out=wqs[:, 2:3], in_=wb16[b][:, 0:fcols],
                                      axis=AXL.X, op=ALU.add,
                                      apply_absolute_value=True)
                    eng.drain()
                    eng.tensor_tensor(out=wqs[:, 3:4], in0=wqs[:, 3:4],
                                      in1=wqs[:, 2:3], op=ALU.add)
                V(f)
                last_vd = inc("v", sd)
                vds[b] = last_vd
            wt("t", sd, last_vd)
            pe_wait_bank("pmisc")

            def f(eng):
                return eng.matmul(out=pmisc[0:1, 0:1], lhsT=wqs[:, 3:4],
                                  rhs=onesf[:], start=True, stop=True)
            vpe = T_inc(f)
            wt("v", spe, vpe)

            def f(eng):
                eng.tensor_copy(out=wqs[0:1, 2:3], in_=pmisc[0:1, 0:1])
            V(f)
            vd2 = inc("v", sd)
            wt("t", sd, vd2)

            def f(eng):
                return eng.matmul(out=pmisc[:, 4:5], lhsT=onesr[:],
                                  rhs=wqs[0:1, 2:3], start=True, stop=True)
            vpe2 = T_inc(f)
            nelem = float(nslab * 128 * fcols)
            wt("v", spe, vpe2)

            def f(eng, nelem=nelem):
                eng.tensor_scalar(out=wqs[:, 2:3], in0=pmisc[:, 4:5],
                                  scalar1=1.0 / nelem, scalar2=EPS,
                                  op0=ALU.mult, op1=ALU.max)
                eng.drain()
                eng.reciprocal(out=wqs[:, 0:1], in_=wqs[:, 2:3])
                eng.drain()
                eng.tensor_copy(out=wqs[:, 1:2], in_=wqs[:, 2:3])
            V(f)
            v_scal = inc("v", sd)
            bank_free["pmisc"] = (sd, v_scal)
            last_store = [None, None, None, None]
            for s_i in range(nslab):
                b = s_i % 4
                gate = last_store[b] if last_store[b] is not None else None
                vw = dma_in(wb16[b][:, 0:fcols], src[s_i], gate=gate)
                wtv("v", vw)
                if s_i == 0:
                    wt("v", sd, v_scal)
                if last_store[(s_i) % 2 + 2] is not None and False:
                    pass
                if s_i >= 2 and last_store[(s_i - 2) % 4] is not None:
                    wtv("v", last_store[(s_i - 2) % 4])

                def f(eng, b=b, fcols=fcols):
                    for p0 in range(0, fcols, TC):
                        wv = wb16[b][:, p0:p0 + TC]
                        eng.tensor_scalar(out=t32a[:], in0=wv, scalar1=wqs[:, 0:1],
                                          scalar2=None, op0=ALU.mult)
                        eng.tensor_copy(out=ti32[:], in_=t32a[:])
                        eng.tensor_tensor(out=t32b[:], in0=t32a[:], in1=ti32[:],
                                          op=ALU.subtract)
                        eng.tensor_scalar(out=t32a[:], in0=t32b[:], scalar1=0.5,
                                          scalar2=None, op0=ALU.is_ge)
                        eng.tensor_scalar(out=t32c[:], in0=t32b[:], scalar1=-0.5,
                                          scalar2=None, op0=ALU.is_le)
                        eng.tensor_tensor(out=t32b[:], in0=ti32[:], in1=t32a[:],
                                          op=ALU.add)
                        eng.tensor_tensor(out=t32b[:], in0=t32b[:], in1=t32c[:],
                                          op=ALU.subtract)
                        eng.tensor_scalar(out=t32b[:], in0=t32b[:], scalar1=1.0,
                                          scalar2=-1.0, op0=ALU.min, op1=ALU.max)
                        eng.tensor_scalar(out=wo16[b % 2][:, p0:p0 + TC],
                                          in0=t32b[:],
                                          scalar1=wqs[:, 1:2], scalar2=None,
                                          op0=ALU.mult)
                V(f)
                vd = inc("v", sd)
                last_store[b] = dma_out(dst[s_i], wo16[b % 2][:, 0:fcols],
                                        gate=(sd, vd))

        raw_bf = {}
        raw_vals = {}
        for wname in ("wg", "wu", "wd"):
            raw_bf[wname] = nc.dram_tensor(f"r_{wname}", [32, 128, 1024], BF16,
                                           kind="Internal").ap()
            raw_vals[wname] = dma_gp(raw_bf[wname], ins[wname])
        class _Every2:
            def __init__(self, ap, off):
                self.ap = ap
                self.off = off

            def __getitem__(self, i):
                return self.ap[2 * i + self.off]

        quant_stage(raw_bf["wg"], _Every2(stg["wgu"], 0), 32, 1024,
                    raw_vals["wg"])
        quant_stage(raw_bf["wu"], _Every2(stg["wgu"], 1), 32, 1024,
                    raw_vals["wu"])
        quant_stage(raw_bf["wd"], stg["wd"], 32, 1024, raw_vals["wd"])
        v_stage_stores = all_vals(ssts)
        v_stage_dved = inc("v", sd)

        def emit_ln_rows(src_tiles, get_x):
            vds = []
            for m in range(NDC):
                for (kk, s_, v_) in get_x(m):
                    wt(kk, s_, v_)

                def f(eng, m=m):
                    ap = src_tiles[m]
                    eng.tensor_tensor(out=t32a[:], in0=ap, in1=ap, op=ALU.mult)
                V(f)
                vd = inc("v", sd)
                vds.append(vd)
                wt("t", sd, vd)
                if m == 0:
                    pe_wait_bank("pmisc")
                    pe_wait_bank("pdr")

                def f(eng, m=m):
                    eng.matmul(out=pmisc[0:1, 0:TC], lhsT=onesf[:],
                               rhs=src_tiles[m], start=(m == 0),
                               stop=(m == NDC - 1))
                    return eng.matmul(out=pdr[0:1, 0:TC], lhsT=onesf[:],
                                      rhs=t32a[:], start=(m == 0),
                                      stop=(m == NDC - 1))
                if m < NDC - 1:
                    T(lambda eng, f=f: f(eng) and None)
                else:
                    vpe = T_inc(f)
            wt("v", spe, vpe)

            def f(eng):
                eng.tensor_scalar(out=rows[1][:], in0=pdr[0:1, 0:TC],
                                  scalar1=1.0 / D, scalar2=None, op0=ALU.mult)
                eng.tensor_scalar(out=rows[0][:], in0=pmisc[0:1, 0:TC],
                                  scalar1=1.0 / D, scalar2=None, op0=ALU.mult)
                eng.tensor_tensor(out=rows[2][:], in0=rows[0][:], in1=rows[0][:],
                                  op=ALU.mult)
                eng.tensor_tensor(out=rows[1][:], in0=rows[1][:], in1=rows[2][:],
                                  op=ALU.subtract)
            V(f)
            vd = inc("v", sd)
            wt("a", sd, vd)

            def f(eng):
                eng.activation(out=rows[2][:], in_=rows[1][:], func=AF.Ln,
                               bias=epscol[0:1, 0:1])
                eng.activation(out=rows[3][:], in_=rows[2][:], func=AF.Exp,
                               scale=c_mh[0:1, 0:1])
            A(f)
            va = inc("a", sa)
            wt("v", sa, va)

            def f(eng):
                eng.tensor_copy(out=rows_bf[0][:], in_=rows[3][:])
                eng.tensor_tensor(out=rows[2][:], in0=rows[0][:], in1=rows[3][:],
                                  op=ALU.mult)
                eng.tensor_scalar(out=rows[2][:], in0=rows[2][:], scalar1=-1.0,
                                  scalar2=None, op0=ALU.mult)
                eng.tensor_copy(out=rows_bf[1][:], in_=rows[2][:])
            V(f)
            vd = inc("v", sd)
            wt("t", sd, vd)
            pe_wait_bank("pcast")

            def f(eng):
                eng.matmul(out=pcast[:, 0:TC], lhsT=ones1[:], rhs=rows_bf[0][:],
                           start=True, stop=True)
                return eng.matmul(out=pmisc[:, 0:TC], lhsT=ones1[:],
                                  rhs=rows_bf[1][:], start=True, stop=True)
            return T_inc(f)

        def dump_hs(dst, gate_val):
            for m in range(NDC):
                dma_out(dst[m * 128:(m + 1) * 128, :], h_s[m][:],
                        gate=(sd, gate_val))

        for c in range(NCH):
            tsl = slice(c * TC, (c + 1) * TC)

            LBL(f"c{c}_ln")
            if c == 0:
                wt("v", sd, v_gates)

            xv = [None] * NDC
            xcons = [None] * NDC

            def emit_x_loads():
                for m in range(NDC):
                    gate = (sd, xcons[m - 2]) if m >= 2 and xcons[m - 2] else None
                    xv[m] = dma_in(xld[m % 2][:], xT[m * 128:(m + 1) * 128, tsl],
                                   gate=gate)

            for m in range(NDC):
                gate = (sd, xcons[m - 2]) if m >= 2 and xcons[m - 2] else None
                xv[m] = dma_in(xld[m % 2][:], xT[m * 128:(m + 1) * 128, tsl],
                               gate=gate)
                wtv("v", xv[m])

                def f(eng, m=m):
                    eng.tensor_copy(out=h_s[m][:], in_=xld[m % 2][:])
                V(f)
                xcons[m] = inc("v", sd)

            v_bc1 = emit_ln_rows([h_s[m][:] for m in range(NDC)],
                                 lambda m: [("v", sd, xcons[m])] if m == 0 else [])
            wt("v", spe, v_bc1)

            def f(eng):
                for m in range(NDC):
                    eng.tensor_tensor(out=t32a[:], in0=h_s[m][:],
                                      in1=pcast[:, 0:TC], op=ALU.mult)
                    eng.tensor_tensor(out=t32a[:], in0=t32a[:], in1=pmisc[:, 0:TC],
                                      op=ALU.add)
                    eng.tensor_scalar(out=h_s[m][:], in0=t32a[:],
                                      scalar1=gates[:, 8 + m:9 + m],
                                      scalar2=gates[:, 0 + m:1 + m],
                                      op0=ALU.mult, op1=ALU.add)
            V(f)
            v_h0 = inc("v", sd)
            bank_free["pmisc"] = (sd, v_h0)
            bank_free["pcast"] = (sd, v_h0)


            for li in range(2):
                LBL(f"c{c}_L{li}_rms")
                for m in range(NDC):
                    def f(eng, m=m):
                        eng.tensor_tensor(out=t32a[:], in0=h_s[m][:],
                                          in1=h_s[m][:], op=ALU.mult)
                    V(f)
                    vd = inc("v", sd)
                    wt("t", sd, vd)
                    if m == 0:
                        pe_wait_bank("pdr")

                    def f(eng, m=m):
                        return eng.matmul(out=pdr[0:1, 0:TC], lhsT=onesf[:],
                                          rhs=t32a[:], start=(m == 0),
                                          stop=(m == NDC - 1))
                    if m < NDC - 1:
                        T(lambda eng, f=f: f(eng) and None)
                    else:
                        vpe = T_inc(f)
                wt("v", spe, vpe)

                def f(eng):
                    eng.tensor_scalar(out=rows[1][:], in0=pdr[0:1, 0:TC],
                                      scalar1=1.0 / D, scalar2=None, op0=ALU.mult)
                V(f)
                vd = inc("v", sd)
                wt("a", sd, vd)

                def f(eng):
                    eng.activation(out=rows[2][:], in_=rows[1][:], func=AF.Ln,
                                   bias=epscol[0:1, 0:1])
                    eng.activation(out=rows[3][:], in_=rows[2][:], func=AF.Exp,
                                   scale=c_mh[0:1, 0:1])
                A(f)
                va = inc("a", sa)
                wt("v", sa, va)

                def f(eng):
                    eng.tensor_copy(out=rows_bf[0][:], in_=rows[3][:])
                V(f)
                vd = inc("v", sd)
                wt("t", sd, vd)
                pe_wait_bank("pcast")

                def f(eng):
                    return eng.matmul(out=pcast[:, 0:TC], lhsT=ones1[:],
                                      rhs=rows_bf[0][:], start=True, stop=True)
                vpe = T_inc(f)
                wt("v", spe, vpe)

                def f(eng, li=li):
                    for m in range(NDC):
                        eng.scalar_tensor_tensor(
                            out=xn_bf[m][:], in0=h_s[m][:],
                            scalar=nw_sb[li][:, m:m + 1], in1=pcast[:, 0:TC],
                            op0=ALU.mult, op1=ALU.mult)
                V(f)
                v_xn = inc("v", sd)
                bank_free["pcast"] = (sd, v_xn)
                bank_free["pdr"] = (sd, v_xn)

                LBL(f"c{c}_L{li}_inproj")
                if c == 0 and li == 0:
                    for s_, v_ in v_stage_stores:
                        wt("v", s_, v_)
                        wt("a", s_, v_)
                pe_done = [None] * 32
                for s_i in range(32):
                    bp = (s_i // 2) % 2
                    pb = s_i % 2
                    if s_i % 2 == 0:
                        gate = (spe, pe_done[s_i - 3]) if s_i >= 4 else None
                        if c == 0 and s_i < 4:
                            gate = plain_vals[2 * li]
                        vw = dma_in(slabA[bp][:].rearrange(
                                        "p (s c) -> p s c", s=2),
                                    stg[f"inw{li}"][s_i:s_i + 2].rearrange(
                                        "s p c -> p s c"), gate=gate)
                        wtv("t", vw)
                    if s_i == 0:
                        wt("t", sd, v_xn)
                    pe_wait_bank(pb)

                    def f(eng, bp=bp, pb=pb, off=s_i % 2):
                        mi = None
                        for k in range(NDC):
                            mi = eng.matmul(out=pbank[pb][:],
                                            lhsT=slabA[bp][:, off * 1024 + k * 128:
                                                           off * 1024 + (k + 1) * 128],
                                            rhs=xn_bf[k][:],
                                            start=(k == 0), stop=(k == NDC - 1))
                        return mi
                    vpe = T_inc(f)
                    pe_done[s_i] = vpe
                    if s_i < NE:
                        ec = s_i
                        wt("a", spe, vpe)

                        def f(eng, ec=ec, pb=pb):
                            xb = xi_buf[ec % 2]
                            eng.copy(out=xb[:, 3:3 + TC], in_=pbank[pb][:])
                        A(f)
                        va_e = inc("a", sa)
                        wt("v", sa, va_e)

                        def f(eng, ec=ec, pb=pb, li=li):
                            xb = xi_buf[ec % 2]
                            eng.tensor_copy(out=xb[:, 0:3],
                                            in_=tails[li][:, ec * 3:ec * 3 + 3])
                            eng.tensor_scalar(
                                out=tb16a[:], in0=xb[:, 0:TC],
                                scalar1=cw_sb[li][:, ec * 4:ec * 4 + 1],
                                scalar2=cb_sb[li][:, ec:ec + 1],
                                op0=ALU.mult, op1=ALU.add)
                            for k in range(1, KCV):
                                eng.scalar_tensor_tensor(
                                    out=tb16a[:], in0=xb[:, k:k + TC],
                                    scalar=cw_sb[li][:, ec * 4 + k:ec * 4 + k + 1],
                                    in1=tb16a[:], op0=ALU.mult, op1=ALU.add)
                            eng.tensor_copy(out=tails[li][:, ec * 3:ec * 3 + 3],
                                            in_=xb[:, TC:TC + 3])
                        V(f)
                        vd = inc("v", sd)
                        bank_free[pb] = (sd, vd)
                        wt("a", sd, vd)

                        def f(eng, ec=ec):
                            eng.activation(out=u_bf[ec], in_=tb16a[:], func=AF.Silu)
                        A(f)
                        inc("a", sa)
                    else:
                        ec = s_i - NE
                        wt("a", spe, vpe)

                        def f(eng, ec=ec, pb=pb):
                            eng.activation(out=zs_bf[ec], in_=pbank[pb][:],
                                           func=AF.Silu)
                        A(f)
                        va = inc("a", sa)
                        bank_free[pb] = (sa, va)
                v_u_all = inc("a", sa)

                LBL(f"c{c}_L{li}_xp")
                wt("t", sa, v_u_all)
                pe_wait_bank("pdr")
                pe_wait_bank("pbc")

                def f(eng, li=li):
                    for k in range(NE):
                        base = k * 96
                        eng.matmul(out=pdr[:, 0:TC],
                                   lhsT=xpw_sb[li][:, base:base + 64],
                                   rhs=u_bf[k], start=(k == 0), stop=(k == NE - 1))
                    mi = None
                    for k in range(NE):
                        base = k * 96
                        mi = eng.matmul(out=pbc[:, 0:TC],
                                        lhsT=xpw_sb[li][:, base + 64:base + 96],
                                        rhs=u_bf[k], start=(k == 0),
                                        stop=(k == NE - 1))
                    return mi
                v_xp = T_inc(f)
                wt("v", spe, v_xp)

                def f(eng):
                    eng.tensor_copy(out=dr_bf[:], in_=pdr[:, 0:TC])
                    eng.tensor_copy(out=bc_bf[:], in_=pbc[:, 0:TC])
                V(f)
                v_drbc = inc("v", sd)
                bank_free["pdr"] = (sd, v_drbc)
                bank_free["pbc"] = (sd, v_drbc)

                wt("t", sd, v_drbc)
                for n in range(2 * NST):
                    pb = 2 + (n % 2)
                    pe_wait_bank(pb)

                    def f(eng, n=n, pb=pb):
                        return eng.matmul(out=pbank[pb][:],
                                          lhsT=oneh_bf[:, n * 128:(n + 1) * 128],
                                          rhs=bc_bf[:], start=True, stop=True)
                    vpe = T_inc(f)
                    dstt = Bbc[n] if n < NST else Cbc[n - NST]
                    if n % 2 == 0:
                        wt("v", spe, vpe)

                        def f(eng, dstt=dstt, pb=pb):
                            eng.tensor_copy(out=dstt, in_=pbank[pb][:])
                        V(f)
                        vd = inc("v", sd)
                        bank_free[pb] = (sd, vd)
                    else:
                        wt("a", spe, vpe)

                        def f(eng, dstt=dstt, pb=pb):
                            eng.copy(out=dstt, in_=pbank[pb][:])
                        A(f)
                        va = inc("a", sa)
                        bank_free[pb] = (sa, va)
                v_bc_a = inc("a", sa)
                v_bc_v = inc("v", sd)

                LBL(f"c{c}_L{li}_scan")
                da_guard = [None] * 2
                dbx_guard = [None] * 2
                g_guard = [None] * 2
                dbufs = dbxs
                gbufs = gbufs4
                first_pool = [True]
                dtbanks = [pmisc, pcast]

                def emit_dt(ec2, li=li):
                    bk = "dtb%d" % (ec2 % 2)
                    pe_wait_bank(bk)
                    if ec2 < 2:
                        pe_wait_bank("pmisc" if ec2 % 2 == 0 else "pcast")
                    if ec2 == 0:
                        wt("t", sd, v_drbc)

                    def f(eng, ec2=ec2, li=li):
                        return eng.matmul(
                            out=dtbanks[ec2 % 2][:, 0:TC],
                            lhsT=dtw_sb[li][:, ec2 * 128:(ec2 + 1) * 128],
                            rhs=dr_bf[:], start=True, stop=True)
                    v_dt = T_inc(f)
                    wt("a", spe, v_dt)

                    def f(eng, li=li, ec2=ec2):
                        eng.activation(out=t32a[:], in_=dtbanks[ec2 % 2][:, 0:TC],
                                       func=AF.Abs,
                                       bias=dtb_sb[li][:, ec2:ec2 + 1])
                        eng.activation(out=t32a[:], in_=t32a[:], func=AF.Exp,
                                       scale=c_m1[:, 0:1])
                        eng.activation(out=t32b[:], in_=t32a[:], func=AF.Ln,
                                       bias=onesf[:, 0:1])
                    A(f)
                    va = inc("a", sa)
                    wt("v", sa, va)

                    def f(eng, li=li, ec2=ec2):
                        eng.tensor_scalar(out=t32c[:], in0=dtbanks[ec2 % 2][:, 0:TC],
                                          scalar1=dtb_sb[li][:, ec2:ec2 + 1],
                                          scalar2=0.0, op0=ALU.add, op1=ALU.max)
                        eng.tensor_tensor(out=dlt2[ec2 % 2][:], in0=t32c[:],
                                          in1=t32b[:], op=ALU.add)
                        eng.tensor_tensor(out=vv2[ec2 % 2][:], in0=dlt2[ec2 % 2][:],
                                          in1=u_bf[ec2], op=ALU.mult)
                    V(f)
                    v_dlt = inc("v", sd)
                    bank_free[bk] = (sd, v_dlt)
                    return v_dlt

                v_dlt_arr = [None] * (NE + 1)
                v_dlt_arr[0] = emit_dt(0)
                for ec in range(NE):
                    if ec + 1 < NE:
                        v_dlt_arr[ec + 1] = emit_dt(ec + 1)
                    v_dlt = v_dlt_arr[ec]
                    dlt = dlt2[ec % 2]
                    vv = vv2[ec % 2]
                    wt("p", sd, v_dlt)
                    if first_pool[0]:
                        wt("p", sa, v_bc_a)
                        wt("p", sd, v_bc_v)
                        first_pool[0] = False
                    yb = "yb%d" % (ec % 2)
                    ybank = pbank[2 + ec % 2]

                    for ng in range(NST // NGRP):
                        for nj in range(NGRP):
                            n = ng * NGRP + nj
                            bidx = n % 2
                            wt("a", sd, v_dlt)
                            if da_guard[bidx] is not None:
                                wt("a", ssc, da_guard[bidx])

                            def f(eng, li=li, ec=ec, n=n, bidx=bidx, dlt=dlt):
                                eng.activation(
                                    out=dA[bidx][:], in_=dlt[:], func=AF.Exp,
                                    scale=acol_sb[li][:, ec * NST + n:
                                                      ec * NST + n + 1])
                            A(f)
                            v_da = inc("a", sdA, hard=False)
                            if dbx_guard[bidx] is not None:
                                wt("p", ssc, dbx_guard[bidx])

                            def f(eng, n=n, bidx=bidx, vv=vv):
                                eng.tensor_tensor(out=dbufs[bidx][:], in0=vv[:],
                                                  in1=Bbc[n], op=ALU.mult)
                            P(f)
                            v_pp = inc("p", spp, hard=False)
                            wt("v", sdA, v_da)
                            wt("v", spp, v_pp)

                            def f(eng, li=li, ec=ec, n=n, nj=nj, bidx=bidx):
                                eng.tensor_tensor_scan(
                                    out=Hbuf[:, nj * TC:(nj + 1) * TC],
                                    data0=dA[bidx][:], data1=dbufs[bidx][:],
                                    initial=st_state[li][:, ec * NST + n:
                                                         ec * NST + n + 1],
                                    op0=ALU.mult, op1=ALU.add)
                            V(f)
                            vsc = inc("v", ssc, hard=False)
                            da_guard[bidx] = vsc
                            dbx_guard[bidx] = vsc
                            if g_guard[bidx] is not None:
                                wt("v", spe, g_guard[bidx])
                            if n == 0:
                                pe_wait_bank(yb)

                            def f(eng, ec=ec, n=n, nj=nj, bidx=bidx):
                                eng.tensor_tensor(out=gbufs[bidx][:],
                                                  in0=Hbuf[:, nj * TC:(nj + 1) * TC],
                                                  in1=Cbc[n], op=ALU.mult)
                            V(f)
                            v_g = inc("v", sgd, hard=False)
                            wt("t", sgd, v_g)

                            def f(eng, n=n, bidx=bidx, ybank=ybank):
                                return eng.matmul(out=ybank[:], lhsT=eye_bf[:],
                                                  rhs=gbufs[bidx][:],
                                                  start=(n == 0),
                                                  stop=(n == NST - 1))
                            g_guard[bidx] = T_inc(f)

                        def f(eng, li=li, ec=ec, ng=ng):
                            hr = Hbuf.rearrange("p (g t) -> p g t", t=TC)
                            eng.drain()
                            eng.tensor_copy(
                                out=st_state[li][:, ec * NST + ng * NGRP:
                                                 ec * NST + ng * NGRP + NGRP],
                                in_=hr[:, :, TC - 1])
                        V(f)

                    wt("v", spe, g_guard[(NST - 1) % 2])
                    def f(eng, li=li, ec=ec):
                        eng.scalar_tensor_tensor(out=tb16a[:], in0=u_bf[ec],
                                                 scalar=dp_sb[li][:, ec:ec + 1],
                                                 in1=pbank[2 + ec % 2][:],
                                                 op0=ALU.mult, op1=ALU.add)
                        eng.tensor_tensor(out=ym_bf[ec], in0=tb16a[:],
                                          in1=zs_bf[ec], op=ALU.mult)
                    V(f)
                    v_yf = inc("v", sd)
                    bank_free["yb%d" % (ec % 2)] = (sd, v_yf)
                v_ym = inc("v", sd)
                for par in range(2):
                    if ("dtb%d" % par) in bank_free:
                        bank_free["pmisc" if par == 0 else "pcast"] = \
                            bank_free.pop("dtb%d" % par)

                LBL(f"c{c}_L{li}_outproj")
                pe_done = [None] * 8
                for s_i in range(8):
                    b = s_i % 2
                    pb = s_i % 2
                    gate = (spe, pe_done[s_i - 2]) if s_i >= 2 else None
                    if c == 0 and s_i < 2:
                        gate = plain_vals[2 * li + 1]
                    vw = dma_in(slabB[b][:], stg[f"outw{li}"][s_i], gate=gate)
                    wtv("t", vw)
                    if s_i == 0:
                        wt("t", sd, v_ym)
                    pe_wait_bank(pb)

                    def f(eng, b=b, pb=pb):
                        mi = None
                        for k in range(NE):
                            mi = eng.matmul(out=pbank[pb][:],
                                            lhsT=slabB[b][:, k * 128:(k + 1) * 128],
                                            rhs=ym_bf[k],
                                            start=(k == 0), stop=(k == NE - 1))
                        return mi
                    vpe = T_inc(f)
                    pe_done[s_i] = vpe
                    wt("v", spe, vpe)

                    def f(eng, m=s_i, pb=pb):
                        eng.tensor_tensor(out=h_s[m][:], in0=h_s[m][:],
                                          in1=pbank[pb][:], op=ALU.add)
                    V(f)
                    vd = inc("v", sd)
                    bank_free[pb] = (sd, vd)


            LBL(f"c{c}_gate")
            xcons = [None] * NDC
            for m in range(NDC):
                gate = (sd, xcons[m - 2]) if m >= 2 and xcons[m - 2] else None
                xv2 = dma_in(xld[m % 2][:], xT[m * 128:(m + 1) * 128, tsl],
                             gate=gate)
                wtv("v", xv2)

                def f(eng, m=m):
                    eng.scalar_tensor_tensor(out=h_s[m][:], in0=h_s[m][:],
                                             scalar=gates[:, 16 + m:17 + m],
                                             in1=xld[m % 2][:],
                                             op0=ALU.mult, op1=ALU.add)
                V(f)
                xcons[m] = inc("v", sd)


            LBL(f"c{c}_ffnln")
            v_bc2 = emit_ln_rows([h_s[m][:] for m in range(NDC)],
                                 lambda m: [])
            wt("v", spe, v_bc2)

            def f(eng):
                for m in range(NDC):
                    eng.tensor_tensor(out=t32a[:], in0=h_s[m][:],
                                      in1=pcast[:, 0:TC], op=ALU.mult)
                    eng.tensor_tensor(out=t32a[:], in0=t32a[:], in1=pmisc[:, 0:TC],
                                      op=ALU.add)
                    eng.tensor_scalar(out=hffn[m], in0=t32a[:],
                                      scalar1=gates[:, 32 + m:33 + m],
                                      scalar2=gates[:, 24 + m:25 + m],
                                      op0=ALU.mult, op1=ALU.add)
                    if m == 0:
                        eng.tensor_copy(out=maxt, in_=hffn[m])
                        eng.tensor_copy(out=maxn, in_=hffn[m])
                    else:
                        eng.tensor_tensor(out=maxt, in0=maxt, in1=hffn[m],
                                          op=ALU.max)
                        eng.tensor_tensor(out=maxn, in0=maxn, in1=hffn[m],
                                          op=ALU.min)
            V(f)
            v_h2 = inc("v", sd)
            bank_free["pcast"] = (sd, v_h2)
            bank_free["pmisc"] = (sd, v_h2)


            def aq_scales(v_ready):
                def f(eng):
                    eng.tensor_scalar(out=t32a[:], in0=maxn, scalar1=-1.0,
                                      scalar2=None, op0=ALU.mult)
                    eng.tensor_tensor(out=maxt, in0=maxt, in1=t32a[:],
                                      op=ALU.max)
                V(f)
                v_ready = inc("v", sd)
                wt("t", sd, v_ready)
                pe_wait_bank("pmisc")

                def f(eng):
                    mi = None
                    for j in range(4):
                        mi = eng.transpose(out=pmisc[:, j * 128:(j + 1) * 128],
                                           in_=maxt[:, j * 128:(j + 1) * 128],
                                           identity=eye_f[:])
                    return mi
                vpe = T_inc(f)
                wt("v", spe, vpe)

                def f(eng):
                    for j in range(4):
                        eng.tensor_reduce(out=mcols[:, j:j + 1],
                                          in_=pmisc[:, j * 128:(j + 1) * 128],
                                          axis=AXL.X, op=ALU.max)
                    eng.drain()
                    eng.tensor_scalar(out=mcols[:, 4:8], in0=mcols[:, 0:4],
                                      scalar1=EPS, scalar2=None, op0=ALU.max)
                    eng.drain()
                    eng.reciprocal(out=mcols[:, 0:4], in_=mcols[:, 4:8])
                    eng.drain()
                    eng.tensor_scalar(out=mcols[:, 0:4], in0=mcols[:, 0:4],
                                      scalar1=127.0, scalar2=None, op0=ALU.mult)
                    eng.drain()
                    eng.tensor_scalar(out=mcols[:, 4:8], in0=mcols[:, 4:8],
                                      scalar1=1.0 / 127.0, scalar2=None,
                                      op0=ALU.mult)
                V(f)
                vd = inc("v", sd)
                bank_free["pmisc"] = (sd, vd)
                wt("t", sd, vd)
                pe_wait_bank("pmisc")

                def f(eng):
                    return eng.transpose(out=pmisc[0:8, 0:128], in_=mcols[:, 0:8],
                                         identity=eye_f[:])
                vpe = T_inc(f)
                wt("v", spe, vpe)

                def f(eng):
                    eng.tensor_copy(out=scl16[0:8, :], in_=pmisc[0:8, 0:128])
                V(f)
                vd = inc("v", sd)
                bank_free["pmisc"] = (sd, vd)
                vs1 = dma_out(rowb[0:8], scl16[0:8, :], gate=(sd, vd))
                vrow = dma_in(rows[0][:],
                              rowb[0:4].rearrange("a b -> (a b)"),
                              gate=vs1)
                vrow2 = dma_in(rows[1][:],
                               rowb[4:8].rearrange("a b -> (a b)"))
                wtv("v", vrow2)

                def f(eng):
                    eng.tensor_copy(out=rows_bf[0][:], in_=rows[0][:])
                    eng.tensor_copy(out=rows_bf[1][:], in_=rows[1][:])
                V(f)
                vd = inc("v", sd)
                wt("t", sd, vd)
                pe_wait_bank("pcast")
                pe_wait_bank("pmisc")

                def f(eng):
                    eng.matmul(out=pcast[:, 0:TC], lhsT=ones1[:], rhs=rows_bf[0][:],
                               start=True, stop=True)
                    return eng.matmul(out=pmisc[:, 0:TC], lhsT=ones1[:],
                                      rhs=rows_bf[1][:], start=True, stop=True)
                vpe = T_inc(f)
                return vpe

            def aq_apply(eng, srct, dstt):
                eng.tensor_tensor(out=t32a[:], in0=srct,
                                  in1=pcast[:, 0:TC], op=ALU.mult)
                eng.tensor_copy(out=ti32[:], in_=t32a[:])
                eng.tensor_tensor(out=t32b[:], in0=t32a[:], in1=ti32[:],
                                  op=ALU.subtract)
                eng.tensor_scalar(out=t32a[:], in0=t32b[:], scalar1=0.5,
                                  scalar2=None, op0=ALU.is_ge)
                eng.tensor_scalar(out=t32c[:], in0=t32b[:], scalar1=-0.5,
                                  scalar2=None, op0=ALU.is_le)
                eng.tensor_tensor(out=t32b[:], in0=ti32[:], in1=t32a[:],
                                  op=ALU.add)
                eng.tensor_tensor(out=t32b[:], in0=t32b[:], in1=t32c[:],
                                  op=ALU.subtract)
                eng.tensor_tensor(out=dstt, in0=t32b[:],
                                  in1=pmisc[:, 0:TC], op=ALU.mult)

            def act_quant(src_tiles, nt, dst_tiles, v_ready):
                vpe = aq_scales(v_ready)
                wt("v", spe, vpe)

                def f(eng, nt=nt):
                    for m in range(nt):
                        aq_apply(eng, src_tiles[m], dst_tiles[m])
                V(f)
                vq = inc("v", sd)
                bank_free["pcast"] = (sd, vq)
                bank_free["pmisc"] = (sd, vq)
                return vq

            LBL(f"c{c}_xq")
            v_xq = act_quant(hffn, NDC, xq_bf, v_h2)

            LBL(f"c{c}_gu")
            if c == 0:
                for s_, v_ in v_stage_stores:
                    def f(eng, s_=s_, v_=v_):
                        eng.wait_ge(s_.h, v_)
                    SY(f)
            pe_done = [None] * NIC
            rs_guard = None
            for s_i in range(NIC):
                b = s_i % 2
                pbg = s_i % 2
                pbu = 2 + (s_i % 2)
                gate = (spe, pe_done[s_i - 2]) if s_i >= 2 else None
                vw = dma_in(slabA[b][:].rearrange("p (s c) -> p s c", s=2),
                            stg["wgu"][2 * s_i:2 * s_i + 2].rearrange(
                                "s p c -> p s c"), gate=gate)
                wtv("t", vw)
                if s_i == 0:
                    wt("t", sd, v_xq)
                pe_wait_bank(pbg)
                pe_wait_bank(pbu)

                def f(eng, b=b, pbg=pbg, pbu=pbu):
                    for k in range(NDC):
                        eng.matmul(out=pbank[pbg][:],
                                   lhsT=slabA[b][:, k * 128:(k + 1) * 128],
                                   rhs=xq_bf[k], start=(k == 0),
                                   stop=(k == NDC - 1))
                    mi = None
                    for k in range(NDC):
                        mi = eng.matmul(out=pbank[pbu][:],
                                        lhsT=slabA[b][:, 1024 + k * 128:
                                                       1024 + (k + 1) * 128],
                                        rhs=xq_bf[k], start=(k == 0),
                                        stop=(k == NDC - 1))
                    return mi
                vpe = T_inc(f)
                pe_done[s_i] = vpe
                wt("a", spe, vpe)
                if rs_guard is not None:
                    wt("a", sd, rs_guard)

                def f(eng, pbg=pbg):
                    eng.activation(out=rs_bf, in_=pbank[pbg][:], func=AF.Relu)
                A(f)
                va = inc("a", sa)
                bank_free[pbg] = (sa, va)
                wt("v", sa, va)

                def f(eng, m=s_i, pbu=pbu):
                    eng.tensor_tensor(out=tb16a[:], in0=rs_bf, in1=rs_bf,
                                      op=ALU.mult)
                    eng.tensor_tensor(out=hffn[m], in0=tb16a[:],
                                      in1=pbank[pbu][:], op=ALU.mult)
                    if m == 0:
                        eng.tensor_copy(out=maxt, in_=hffn[m])
                        eng.tensor_copy(out=maxn, in_=hffn[m])
                    else:
                        eng.tensor_tensor(out=maxt, in0=maxt, in1=hffn[m],
                                          op=ALU.max)
                        eng.tensor_tensor(out=maxn, in0=maxn, in1=hffn[m],
                                          op=ALU.min)
                V(f)
                vd = inc("v", sd)
                rs_guard = vd
                bank_free[pbu] = (sd, vd)
            v_hmax = inc("v", sd)

            LBL(f"c{c}_hq")
            v_hqsc = aq_scales(v_hmax)

            LBL(f"c{c}_down")
            hq_ready = [None] * NIC
            out_guard = [None, None]
            for half in range(2):
                pe_done = [None] * NIC
                for k in range(NIC):
                    bp = (k // 2) % 2
                    if k % 2 == 0:
                        gate = (spe, pe_done[k - 3]) if k >= 4 else None
                        vw = dma_in(slabB[bp][:].rearrange(
                                        "p (s c) -> p s c", s=2),
                                    stg["wd"][k:k + 2].rearrange(
                                        "s p c -> p s c"), gate=gate)
                    if half == 0:
                        if k == 0:
                            wt("v", spe, v_hqsc)

                        def f(eng, k=k):
                            aq_apply(eng, hffn[k], hq_bf[k])
                        V(f)
                        hq_ready[k] = inc("v", sd)
                    if k % 2 == 0:
                        wtv("t", vw)
                    wt("t", sd, hq_ready[k])
                    if k == 0:
                        for mh in range(4):
                            pe_wait_bank(mh)

                    def f(eng, k=k, bp=bp, half=half, off=k % 2):
                        mi = None
                        for mh in range(4):
                            m = half * 4 + mh
                            mi = eng.matmul(out=pbank[mh][:],
                                            lhsT=slabB[bp][:, off * 1024 + m * 128:
                                                           off * 1024 + (m + 1) * 128],
                                            rhs=hq_bf[k], start=(k == 0),
                                            stop=(k == NIC - 1))
                        return mi
                    pe_done[k] = T_inc(f)
                vpe_half = pe_done[NIC - 1]
                for mh in range(4):
                    m = half * 4 + mh
                    wt("v", spe, vpe_half)
                    if out_guard[m % 2] is not None:
                        wtv("v", out_guard[m % 2])

                    def f(eng, m=m, mh=mh):
                        dst = t32a if m % 2 == 0 else t32b
                        eng.scalar_tensor_tensor(out=dst[:], in0=pbank[mh][:],
                                                 scalar=gates[:, 40 + m:41 + m],
                                                 in1=h_s[m][:], op0=ALU.mult,
                                                 op1=ALU.add)
                    V(f)
                    vd = inc("v", sd)
                    bank_free[mh] = (sd, vd)
                    out_guard[m % 2] = dma_out(
                        out_d[m * 128:(m + 1) * 128, tsl],
                        (t32a if m % 2 == 0 else t32b)[:], gate=(sd, vd))
            vq_end = inc("v", sd)
            bank_free["pcast"] = (sd, vq_end)
            bank_free["pmisc"] = (sd, vq_end)

        total_stores = all_vals(ssts)

        def f(eng):
            for s_, v_ in total_stores:
                eng.wait_ge(s_.h, v_)
        SY(f)

        class EngProxy:
            def __init__(self, eng):
                self._e = eng
                self.last = None

            def __getattr__(self, name):
                a = getattr(self._e, name)
                if callable(a):
                    track = not (name.startswith("wait") or name == "sem_inc")

                    def w(*args, _a=a, _t=track, **kw):
                        r = _a(*args, **kw)
                        if _t and r is not None and hasattr(r, "then_inc"):
                            self.last = r
                        return r
                    return w
                return a

        from contextlib import ExitStack as _ES

        def _replay(kind, eng):
            p = EngProxy(eng)
            cur = [None]
            with _ES() as st_:
                for k, fn, lbl in ops:
                    if k != kind:
                        continue
                    if lbl != cur[0]:
                        st_.pop_all().close()
                        st_ = _ES()
                        st_.enter_context(nc.named_scope(lbl))
                        cur[0] = lbl
                    fn(p)
                st_.pop_all().close()

        @block.sync
        def _(eng):
            _replay("s", eng)

        @block.vector
        def _(eng):
            _replay("v", eng)

        @block.scalar
        def _(eng):
            _replay("a", eng)

        @block.tensor
        def _(eng):
            _replay("t", eng)

        @block.gpsimd
        def _(eng):
            _replay("p", eng)

    return nc



def _cols(vec, p=128):
    k = vec.shape[0] // p
    return np.ascontiguousarray(vec.reshape(k, p).T)


def _slab_major(wT, nslab):
    K, M = wT.shape
    nk = K // 128
    ms = M // nslab
    a = wT.reshape(nk, 128, nslab, ms)
    a = a.transpose(2, 1, 0, 3).reshape(nslab, 128, nk * ms)
    return np.ascontiguousarray(a)


def kernel(L0_attn_x, L0_attn_in_proj_w, L0_attn_conv_w, L0_attn_conv_b,
           L0_attn_x_proj_w, L0_attn_dt_w, L0_attn_dt_b, L0_attn_A_log,
           L0_attn_D_p, L0_attn_out_proj_w, L0_attn_norm_w, L0_ffn_w_gate,
           L0_ffn_w_up, L0_ffn_w_down, L0_c, L0_wmod):
    f32 = np.float32
    common = {}
    for li in range(2):
        common[f"inw{li}"] = _slab_major(
            np.ascontiguousarray(np.asarray(L0_attn_in_proj_w)[li].T.astype(f32)), 32)
        common[f"outw{li}"] = _slab_major(
            np.ascontiguousarray(np.asarray(L0_attn_out_proj_w)[li].T.astype(f32)), 8)
        xpwT = np.ascontiguousarray(np.asarray(L0_attn_x_proj_w)[li].T.astype(f32))
        common[f"xpw{li}"] = np.ascontiguousarray(
            xpwT.reshape(16, 128, 96).transpose(1, 0, 2).reshape(128, 16 * 96))
        common[f"dtw{li}"] = np.ascontiguousarray(np.asarray(L0_attn_dt_w)[li].T.astype(f32))
        cw = np.asarray(L0_attn_conv_w)[li, :, 0, :].astype(f32)
        common[f"cw{li}"] = np.ascontiguousarray(
            cw.reshape(16, 128, 4).transpose(1, 0, 2).reshape(128, 64))
        common[f"cb{li}"] = _cols(np.asarray(L0_attn_conv_b)[li].astype(f32))
        common[f"dtb{li}"] = _cols(np.asarray(L0_attn_dt_b)[li].astype(f32))
        al = np.asarray(L0_attn_A_log)[li].astype(f32)
        common[f"alog{li}"] = np.ascontiguousarray(
            al.reshape(16, 128, 16).transpose(1, 0, 2).reshape(128, 256))
        common[f"dp{li}"] = _cols(np.asarray(L0_attn_D_p)[li].astype(f32))
        common[f"nw{li}"] = _cols(np.asarray(L0_attn_norm_w)[li].astype(f32))
    common["wg"] = _slab_major(np.ascontiguousarray(np.asarray(L0_ffn_w_gate).T.astype(f32)), 32)
    common["wu"] = _slab_major(np.ascontiguousarray(np.asarray(L0_ffn_w_up).T.astype(f32)), 32)
    common["wd"] = np.ascontiguousarray(
        np.asarray(L0_ffn_w_down).T.astype(f32).reshape(32, 128, 1024))
    common["wmod"] = np.ascontiguousarray(np.asarray(L0_wmod).astype(f32))
    common["eye"] = np.eye(128, dtype=f32)
    common["eyebf"] = np.eye(128, dtype=ml_dtypes.bfloat16)
    oneh = np.zeros((32, 32 * 128), dtype=ml_dtypes.bfloat16)
    for n in range(32):
        oneh[n, n * 128:(n + 1) * 128] = 1
    common["onehbf"] = oneh

    nc = build_nc()
    in_maps = []
    for b in range(8):
        m = dict(common)
        m["xT"] = np.ascontiguousarray(np.asarray(L0_attn_x)[b].astype(f32).T)
        m["ccol"] = _cols(np.asarray(L0_c)[b].astype(f32))
        in_maps.append(m)
    res = run_bass_kernel_spmd(nc, in_maps, core_ids=list(range(8)))
    outs = [np.asarray(r["out"], dtype=np.float32).T for r in res.results]
    return np.ascontiguousarray(np.stack(outs, axis=0))
